# version 29
# baseline (speedup 1.0000x reference)
import contextlib
import numpy as np

import concourse.bass as bass
import concourse.mybir as mybir
from concourse.bass_utils import run_bass_kernel_spmd
from concourse.alu_op_type import AluOpType as OP

F32 = mybir.dt.float32
BF16 = mybir.dt.bfloat16
I32 = mybir.dt.int32
AF = mybir.ActivationFunctionType

S = 2048
D = 1024
NH = 16
HD = 64
LH = 8
LD = 512
NLAT = 64
CH = 32
NQT = 16
NEG = -30000.0
KR = 4
XR = 4
RW = 4
RG = 3

DBG = False
SCHED = "il"
ENGINES = ("pe", "dve", "act", "gps", "sp")
QA = ("qA0", "qA1")
QD = ("qD0", "qD1")
QI = ("qI0", "qI1")


class Node:
    __slots__ = ("eng", "fn", "waits", "inc_val", "is_dma", "seq", "dbg", "eng_pos",
                 "stream")

    def __init__(self, eng, fn, seq, dbg):
        self.eng = eng
        self.fn = fn
        self.waits = {}
        self.inc_val = None
        self.is_dma = False
        self.seq = seq
        self.dbg = dbg
        self.eng_pos = 0
        self.stream = None


class Plan:
    GAP = 10**9

    def __init__(self):
        self.nodes = []
        self.by_eng = {e: [] for e in ENGINES}
        self.last_w = {}
        self.readers = {}
        self.stream_last = {}

    def add(self, eng, fn, reads=(), writes=(), is_dma=False, dbg=""):
        n = Node(eng, fn, len(self.nodes), dbg)
        n.is_dma = is_dma
        n.eng_pos = len(self.by_eng[eng])
        deps = []
        for b in reads:
            w = self.last_w.get(b)
            if w is not None:
                deps.append(w)
        for b in writes:
            w = self.last_w.get(b)
            if w is not None:
                deps.append(w)
            deps.extend(self.readers.get(b, ()))
        for d in deps:
            if d is n:
                continue
            if d.eng == eng and not d.is_dma and n.eng_pos - d.eng_pos > self.GAP:
                continue
            if id(d) not in n.waits:
                n.waits[id(d)] = d
        for b in reads:
            self.readers.setdefault(b, []).append(n)
        for b in writes:
            self.last_w[b] = n
            self.readers[b] = []
        self.nodes.append(n)
        self.by_eng[eng].append(n)
        return n


def build_program():
    nc = bass.Bass("TRN2", target_bir_lowering=False, detect_race_conditions=False)

    x_hbm = nc.dram_tensor("x", [S, D], F32, kind="ExternalInput").ap()
    wq_h = nc.dram_tensor("wq", [D, D], F32, kind="ExternalInput").ap()
    wk_h = nc.dram_tensor("wk", [D, D], F32, kind="ExternalInput").ap()
    wv_h = nc.dram_tensor("wv", [D, D], F32, kind="ExternalInput").ap()
    wgq_h = nc.dram_tensor("wgq", [D, LD], F32, kind="ExternalInput").ap()
    wgk_h = nc.dram_tensor("wgk", [D, LD], F32, kind="ExternalInput").ap()
    wgv_h = nc.dram_tensor("wgv", [D, LD], F32, kind="ExternalInput").ap()
    rout_h = nc.dram_tensor("rout", [LD, D], F32, kind="ExternalInput").ap()
    wout_h = nc.dram_tensor("wout", [D, D], F32, kind="ExternalInput").ap()
    gwbd_h = nc.dram_tensor("gwbd", [D, 32], F32, kind="ExternalInput").ap()
    packed_h = nc.dram_tensor("packed", [262144], I32, kind="ExternalInput").ap()
    ws_h = nc.dram_tensor("wscale", [1], F32, kind="ExternalInput").ap()
    bias_h = nc.dram_tensor("fbias", [D], F32, kind="ExternalInput").ap()
    ident_h = nc.dram_tensor("ident", [128, 128], F32, kind="ExternalInput").ap()
    m0_h = nc.dram_tensor("m0", [128, 384], BF16, kind="ExternalInput").ap()
    m1_h = nc.dram_tensor("m1", [128, 384], BF16, kind="ExternalInput").ap()
    m2_h = nc.dram_tensor("m2", [128, 384], BF16, kind="ExternalInput").ap()
    zeros_h = nc.dram_tensor("zeros", [128, 1024], F32, kind="ExternalInput").ap()
    cvec_h = nc.dram_tensor("cvec", [128, 8], F32, kind="ExternalInput").ap()
    lmasks_h = nc.dram_tensor("lmasks", [2048, 64], BF16, kind="ExternalInput").ap()
    y_hbm = nc.dram_tensor("y", [S, D], F32, kind="ExternalOutput").ap()
    if DBG:
        dbgfm_h = nc.dram_tensor("dbgfm", [3 * 16384, 128], F32, kind="ExternalOutput").ap()
        dbgtm_h = nc.dram_tensor("dbgtm", [4 * S, D], F32, kind="ExternalOutput").ap()
        dbgg_h = nc.dram_tensor("dbgg", [S, 32], F32, kind="ExternalOutput").ap()

    es = contextlib.ExitStack()
    sb = lambda name, shape, dt=F32: es.enter_context(nc.sbuf_tensor(name, shape, dt))
    ps = lambda name: es.enter_context(nc.psum_tensor(name, [128, 512], F32))

    with es:
        h_fm = [sb(f"hfm{c}", [128, 256]) for c in range(8)]
        q_fm = [sb(f"qfm{c}", [128, 256]) for c in range(8)]
        gq = [sb(f"gq{c}", [128, 256]) for c in range(4)]
        kb = [sb(f"kb{c}", [128, 128 * KR]) for c in range(8)]
        vb = [sb(f"vb{s}", [128, D]) for s in range(KR)]
        wout_s = [sb(f"wouts{k}", [128, D]) for k in range(8)]
        rout_s = [sb(f"routs{k}", [128, D]) for k in range(4)]
        tern_s = [sb(f"terns{k}", [128, D], BF16) for k in range(8)]
        ring_w = [sb(f"ringw{i}", [128, D]) for i in range(RW)]
        ring_g = [sb(f"ringg{i}", [128, LD]) for i in range(RG)]
        pooled = [sb(f"pool{c}", [128, NLAT]) for c in range(8)]
        gk = [sb(f"gk{c}", [128, NLAT]) for c in range(4)]
        gvd = sb("gvd", [128, LD])
        gw_s = sb("gws", [128, 256])
        maskv = sb("maskv", [128, 384], BF16)
        mask2 = sb("mask2t", [128, 384], BF16)
        lms = sb("lms", [128, 1024], BF16)
        ident = sb("identt", [128, 128])
        bias_bc = sb("biasbc", [128, D])
        ws_col = sb("wscol", [128, 1])
        gT = [sb(f"gT{i}", [128, 32]) for i in range(4)]
        x_ring = [sb(f"xr{i}", [128, D]) for i in range(XR)]
        h_tm = sb("htm", [128, D])
        PTs = [sb(f"PTs{i}", [128, 384]) for i in range(2)]
        lp = sb("lp", [128, 512])
        lpT3 = [sb(f"lpT{c}", [128, 128]) for c in range(2)]
        lpT = lpT3 + [None, None]
        rem_fm = sb("remfm", [128, 512])
        mix_tm = sb("mixtm", [128, D])
        mix_fm = sb("mixfm", [128, D])
        attnout = mix_fm
        y1 = sb("y1", [128, D])
        qA = sb("qA", [128, D])
        qD = sb("qD", [128, D])
        qI = sb("qI", [128, D], I32)
        kfm = [sb(f"kfm{k}", [128, 128], BF16) for k in range(8)]
        tp = qI[:]
        tfw = h_tm[:]
        tern_om = qA
        SMW = 72
        sm = sb("small", [128, SMW])
        smc = lambda i: sm[:, i:i + 1]
        cvec = sb("cvec_s", [128, 8])
        c0125 = cvec[:, 0:1]
        cm24 = cvec[:, 1:2]
        cm12 = cvec[:, 2:3]
        cmhalf = cvec[:, 3:4]
        cm1 = cvec[:, 5:6]
        c1 = cvec[:, 6:7]

        psum = [ps(f"pp{i}") for i in range(8)]

        STREAMS = tuple(f"rw{i}" for i in range(RW)) + \
            tuple(f"rg{i}" for i in range(RG)) + ("mi0", "mi1", "st", "db0", "db1")
        sems = {}
        for e in ("pe", "dve", "act", "gps"):
            sems[e] = es.enter_context(nc.semaphore(f"sem_{e}"))
        for s_ in STREAMS:
            sems[s_] = es.enter_context(nc.semaphore(f"sem_{s_}"))

        plan = Plan()
        P = plan.add
        mi_rr = [0]

        def dma_node(eng, stream, fn, reads, writes, dbg):
            n = P(eng, fn, reads=reads, writes=writes, is_dma=True, dbg=dbg)
            n.stream = stream
            prev = plan.stream_last.get(stream)
            if prev is not None:
                n.waits[id(prev)] = prev
            plan.stream_last[stream] = n
            return n

        def dma_load(buf_keys, out_ap, in_ap, extra_reads=(), stream=None):
            if isinstance(buf_keys, str):
                buf_keys = (buf_keys,)
            if stream is None:
                stream = f"mi{mi_rr[0] % 2}"
                mi_rr[0] += 1

            def f(E, out_ap=out_ap, in_ap=in_ap):
                return E.dma_start(out=out_ap, in_=in_ap)
            return dma_node("sp", stream, f, extra_reads, buf_keys,
                            f"load {buf_keys[0]}")

        db_rr = [0]

        def dma_store(in_key, out_ap, in_ap, stream="st"):
            def f(E, out_ap=out_ap, in_ap=in_ap):
                return E.dma_start(out=out_ap, in_=in_ap)
            return dma_node("act", stream, f, (in_key,), (), f"store {in_key}")

        def dbg_store(in_key, out_ap, in_ap):
            s_ = f"db{db_rr[0] % 2}"
            db_rr[0] += 1
            return dma_store(in_key, out_ap, in_ap, stream=s_)

        dma_load("ident", ident[:], ident_h[:, :])
        dma_load("maskv", maskv[:], m0_h[:, :])
        dma_load("mask2", mask2[:], m2_h[:, :])
        for k in range(8):
            dma_load("gws", gw_s[:, 32 * k:32 * (k + 1)], gwbd_h[128 * k:128 * (k + 1), :])
        dma_load("biasbc", bias_bc[:], bass.AP(bias_h.tensor, 0, [[0, 128], [1, D]]))
        dma_load("wscol", ws_col[:], bass.AP(ws_h.tensor, 0, [[0, 128], [1, 1]]))
        dma_load("cvec", cvec[:], cvec_h[:, :])
        for g in range(16):
            dma_load("lms", lms[:, 64 * g:64 * (g + 1)], lmasks_h[128 * g:128 * (g + 1), :])
        for k in range(8):
            dma_load(f"wouts{k}", wout_s[k][:], wout_h[128 * k:128 * (k + 1), :])
        for k in range(4):
            dma_load(f"routs{k}", rout_s[k][:], rout_h[128 * k:128 * (k + 1), :])
        for c in range(8):
            dma_load(f"pool{c}", pooled[c][:], zeros_h[:, 0:NLAT])
        for c in range(4):
            dma_load(f"gk{c}", gk[c][:], zeros_h[:, 0:NLAT])
        dma_load("gvd", gvd[:], zeros_h[:, 0:LD])
        for c in range(8):
            for s_ in (2, 3):
                dma_load(f"kb{c}", kb[c][:, 128 * s_:128 * (s_ + 1)], zeros_h[:, 0:128])
        for s_ in (2, 3):
            dma_load(f"vb{s_}", vb[s_][:], zeros_h[:, :])

        for d in range(8):
            dma_load(("tp",) + QI, tp,
                     bass.AP(packed_h.tensor, 131072 * (d % 2), [[1024, 128], [1, 1024]]))
            sh = 2 * (d // 2)
            if sh:
                P("dve", lambda E, sh=sh: E.tensor_scalar(tp, tp, sh, None, OP.logical_shift_right),
                  reads=("tp",) + QI, writes=("tp",) + QI, dbg="unpack shift")
            P("dve", lambda E: E.tensor_scalar(tp, tp, 3, None, OP.bitwise_and),
              reads=("tp",) + QI, writes=("tp",) + QI, dbg="unpack and")
            P("dve", lambda E: E.tensor_copy(tfw, tp),
              reads=("tp",) + QI, writes=("tfw", "htm"), dbg="unpack i2f")
            P("dve", lambda E: E.tensor_scalar(tern_om[:], tfw, 1.0, None, OP.subtract),
              reads=("tfw", "htm"), writes=("tom",) + QA, dbg="unpack sub")
            for c in range(8):
                bank = 2 + c % 2
                P("pe", lambda E, c=c, d=d, bank=bank: E.matmul(
                    psum[bank][:, 0:128], tern_om[:, 128 * c:128 * (c + 1)],
                    ident[:], is_transpose=True, start=True, stop=True),
                  reads=("tom", "ident") + QA, writes=(f"ps{bank}",), dbg="ternT transpose")
                P("act", lambda E, c=c, d=d, bank=bank: E.copy(
                    tern_s[c][:, 128 * d:128 * (d + 1)], psum[bank][:, 0:128]),
                  reads=(f"ps{bank}",), writes=(f"terns{c}",), dbg="ternT evac")

        widx = [0]
        gidx = [0]

        def wload(src_ap):
            i = widx[0] % RW
            widx[0] += 1
            dma_load(f"ringw{i}", ring_w[i][:], src_ap, stream=f"rw{i}")
            return i, f"ringw{i}"

        def gload(src_ap):
            i = gidx[0] % RG
            gidx[0] += 1
            dma_load(f"ringg{i}", ring_g[i][:], src_ap, stream=f"rg{i}")
            return i, f"ringg{i}"

        def ln_rs_bb(src_key, src_ap, dump_keys, dump_ap, i_s1, i_s2, i_rs, i_bb):
            s1, s2, rs, bb = smc(i_s1), smc(i_s2), smc(i_rs), smc(i_bb)
            P("dve", lambda E: E.reduce_sum(s1, src_ap, axis=mybir.AxisListType.X),
              reads=(src_key,), writes=(f"sm{i_s1}",), dbg="ln s1")
            P("act", lambda E: E.activation(dump_ap, src_ap, AF.Square),
              reads=(src_key,), writes=dump_keys, dbg="ln sq")
            P("dve", lambda E: E.reduce_sum(s2, dump_ap, axis=mybir.AxisListType.X),
              reads=dump_keys, writes=(f"sm{i_s2}",), dbg="ln s2")
            P("dve", lambda E: E.tensor_scalar(s1, s1, 1.0 / D, None, OP.mult),
              reads=(f"sm{i_s1}",), writes=(f"sm{i_s1}",), dbg="ln m")
            P("dve", lambda E: E.scalar_tensor_tensor(rs, s1, 1.0, s1, OP.mult, OP.mult),
              reads=(f"sm{i_s1}",), writes=(f"sm{i_rs}",), dbg="ln m2")
            P("dve", lambda E: E.scalar_tensor_tensor(s2, s2, 1.0 / D, rs, OP.mult, OP.subtract),
              reads=(f"sm{i_s2}", f"sm{i_rs}"), writes=(f"sm{i_s2}",), dbg="ln var")
            P("dve", lambda E: E.tensor_scalar(s2, s2, 1.0e-5, None, OP.add),
              reads=(f"sm{i_s2}",), writes=(f"sm{i_s2}",), dbg="ln +eps")
            P("act", lambda E: E.activation(rs, s2, AF.Ln),
              reads=(f"sm{i_s2}",), writes=(f"sm{i_rs}",), dbg="ln ln")
            P("act", lambda E: E.activation(rs, rs, AF.Exp, scale=cmhalf),
              reads=(f"sm{i_rs}", "cvec"), writes=(f"sm{i_rs}",), dbg="ln rsqrt")
            P("dve", lambda E: E.scalar_tensor_tensor(bb, s1, -1.0, rs, OP.mult, OP.mult),
              reads=(f"sm{i_s1}", f"sm{i_rs}"), writes=(f"sm{i_bb}",), dbg="ln bb")
            return rs, bb

        def fe_A_ln(g):
            xr = x_ring[g % XR]
            xkey = f"xr{g % XR}"
            dma_load(xkey, xr[:], x_hbm[g * 128:(g + 1) * 128, :])
            rs, bb = ln_rs_bb(xkey, xr[:], ("htm", "tfw"), h_tm[:], 0, 1, 2, 3)
            P("dve", lambda E, xr=xr, rs=rs, bb=bb: E.tensor_scalar(
                h_tm[:], xr[:], rs, bb, OP.mult, OP.add),
              reads=(xkey, "sm2", "sm3"), writes=("htm", "tfw"), dbg="ln1 apply")

        def fe_A_hT(g):
            col = (g % 2) * 128
            for half in range(2):
                bank = 2 + half
                for r in range(4):
                    c = half * 4 + r
                    P("pe", lambda E, c=c, bank=bank, r=r: E.matmul(
                        psum[bank][:, r * 128:(r + 1) * 128], h_tm[:, 128 * c:128 * (c + 1)],
                        ident[:], is_transpose=True, start=(r == 0), stop=(r == 3)),
                      reads=("htm", "ident"), writes=(f"ps{bank}",), dbg="hT")

                def evac(E, half=half, bank=bank, col=col):
                    last = None
                    for r in range(4):
                        c = half * 4 + r
                        last = E.copy(h_fm[c][:, col:col + 128],
                                      psum[bank][:, r * 128:(r + 1) * 128])
                    return last
                P("act", evac, reads=(f"ps{bank}",),
                  writes=tuple(f"hfm{half*4+r}" for r in range(4)), dbg="hT evac")
            if DBG:
                for c in range(8):
                    r0 = (g * 8 + c) * 128
                    dbg_store(f"hfm{c}", dbgfm_h[r0:r0 + 128, :], h_fm[c][:, col:col + 128])
            for c in range(8):
                lat0 = 4 * g
                P("dve", lambda E, c=c, col=col, lat0=lat0: E.reduce_sum(
                    pooled[c][:, lat0:lat0 + 4],
                    h_fm[c][:, col:col + 128].rearrange("p (s q) -> p s q", q=32),
                    axis=mybir.AxisListType.X),
                  reads=(f"hfm{c}",), writes=(f"pool{c}",), dbg="pool reduce")
                P("dve", lambda E, c=c, lat0=lat0: E.tensor_scalar(
                    pooled[c][:, lat0:lat0 + 4], pooled[c][:, lat0:lat0 + 4],
                    1.0 / CH, None, OP.mult),
                  reads=(f"pool{c}",), writes=(f"pool{c}",), dbg="pool scale")
            if g % 2 == 1:
                lat0 = 4 * (g - 1)
                for k in range(8):
                    i, keyg = gload(wgk_h[128 * k:128 * (k + 1), :])
                    for m in range(4):
                        P("pe", lambda E, i=i, k=k, m=m, lat0=lat0: E.matmul(
                            psum[0][:, 8 * m:8 * m + 8], ring_g[i][:, 128 * m:128 * (m + 1)],
                            pooled[k][:, lat0:lat0 + 8],
                            start=(k == 0 and m == 0), stop=(k == 7 and m == 3)),
                          reads=(keyg, f"pool{k}"), writes=("ps0",), dbg="GK mm")
                for m in range(4):
                    P("act", lambda E, m=m, lat0=lat0: E.copy(gk[m][:, lat0:lat0 + 8],
                                                              psum[0][:, 8 * m:8 * m + 8]),
                      reads=("ps0",), writes=(f"gk{m}",), dbg="GK evac")
                for k in range(8):
                    i, keyg = gload(wgv_h[128 * k:128 * (k + 1), :])
                    P("pe", lambda E, i=i, k=k: E.matmul(
                        psum[1][0:64, :], pooled[k][:, :], ring_g[i][:, :],
                        start=(k == 0), stop=(k == 7)),
                      reads=(keyg, f"pool{k}"), writes=("ps1",), dbg="GV mm")
                P("act", lambda E: E.copy(gvd[0:64, :], psum[1][0:64, :]),
                  reads=("ps1",), writes=("gvd",), dbg="GV evac")
                dma_load("gvd", gvd[64:128, :], gvd[0:64, :], extra_reads=("gvd",))

        def fe_A(g):
            fe_A_ln(g)
            fe_A_hT(g)

        def proj_pair(w_h, out_cb, dbgname):
            for k in range(8):
                i, keyw = wload(w_h[128 * k:128 * (k + 1), :])
                for m in range(8):
                    bk = m % 4
                    reg = 256 * (m // 4)
                    P("pe", lambda E, i=i, k=k, m=m, bk=bk, reg=reg: E.matmul(
                        psum[bk][:, reg:reg + 256], ring_w[i][:, 128 * m:128 * (m + 1)],
                        h_fm[k][:, 0:256],
                        start=(k == 0 and m < 4), stop=(k == 7 and m >= 4)),
                      reads=(keyw, f"hfm{k}"), writes=(f"ps{bk}",), dbg=dbgname)
            for m in range(8):
                out_cb(m, m % 4, 256 * (m // 4))

        def fe_Qpair(b0):
            def out_cb(m, bk, reg):
                P("act", lambda E, m=m, bk=bk, reg=reg: E.copy(
                    q_fm[m][:, 0:256], psum[bk][:, reg:reg + 256]),
                  reads=(f"ps{bk}",), writes=(f"qfm{m}",), dbg="Q evac")
            proj_pair(wq_h, out_cb, "Q mm")
            if DBG:
                for qi in range(2):
                    for c in range(8):
                        r0 = 16384 + ((b0 + qi) * 8 + c) * 128
                        dbg_store(f"qfm{c}", dbgfm_h[r0:r0 + 128, :],
                                  q_fm[c][:, 128 * qi:128 * qi + 128])

        def fe_KVpair(b0):
            s0 = b0 % KR

            def out_cb(m, bk, reg):
                P("act", lambda E, m=m, bk=bk, reg=reg: E.copy(
                    kb[m][:, 128 * s0:128 * s0 + 256], psum[bk][:, reg:reg + 256]),
                  reads=(f"ps{bk}",), writes=(f"kb{m}",), dbg="K evac")
            proj_pair(wk_h, out_cb, "K mm")
            if DBG:
                for qi in range(2):
                    for c in range(8):
                        r0 = 2 * 16384 + ((b0 + qi) * 8 + c) * 128
                        dbg_store(f"kb{c}", dbgfm_h[r0:r0 + 128, :],
                                  kb[c][:, 128 * (s0 + qi):128 * (s0 + qi) + 128])
            for k in range(8):
                i, keyw = wload(wv_h[128 * k:128 * (k + 1), :])
                for qi in range(2):
                    for half in range(2):
                        bk = qi * 2 + half
                        P("pe", lambda E, i=i, k=k, qi=qi, half=half, bk=bk: E.matmul(
                            psum[bk][:, :], h_fm[k][:, 128 * qi:128 * (qi + 1)],
                            ring_w[i][:, 512 * half:512 * (half + 1)],
                            start=(k == 0), stop=(k == 7)),
                          reads=(keyw, f"hfm{k}"), writes=(f"ps{bk}",), dbg="V mm")
            for qi in range(2):
                slot = (b0 + qi) % KR
                for half in range(2):
                    bk = qi * 2 + half
                    P("act", lambda E, half=half, slot=slot, bk=bk: E.copy(
                        vb[slot][:, 512 * half:512 * (half + 1)], psum[bk][:, :]),
                      reads=(f"ps{bk}",), writes=(f"vb{slot}",), dbg="V evac")
                if DBG:
                    g = b0 + qi
                    dbg_store(f"vb{slot}", dbgtm_h[g * 128:(g + 1) * 128, :], vb[slot][:])

        def fe_GQgpair(b0):
            for k in range(8):
                i, keyg = gload(wgq_h[128 * k:128 * (k + 1), :])
                for m in range(4):
                    P("pe", lambda E, i=i, k=k, m=m: E.matmul(
                        psum[m][:, 0:256], ring_g[i][:, 128 * m:128 * (m + 1)],
                        h_fm[k][:, 0:256], start=(k == 0), stop=(k == 7)),
                      reads=(keyg, f"hfm{k}"), writes=(f"ps{m}",), dbg="GQ mm")
            for m in range(4):
                P("act", lambda E, m=m: E.copy(gq[m][:, 0:256], psum[m][:, 0:256]),
                  reads=(f"ps{m}",), writes=(f"gq{m}",), dbg="GQ evac")
            for qi in range(2):
                g = b0 + qi
                col = 128 * qi
                for k in range(8):
                    P("pe", lambda E, k=k, col=col: E.matmul(
                        psum[5][:, 0:32], q_fm[k][:, col:col + 128],
                        gw_s[:, 32 * k:32 * (k + 1)], start=(k == 0), stop=(k == 7)),
                      reads=(f"qfm{k}", "gws"), writes=("ps5",), dbg="gates mm")
                gt = gT[g % 4]
                gkey = f"gT{g % 4}"
                P("act", lambda E, gt=gt: E.activation(gt[:], psum[5][:, 0:32], AF.Exp, scale=cm1),
                  reads=("ps5", "cvec"), writes=(gkey,), dbg="gates exp")
                P("dve", lambda E, gt=gt: E.tensor_scalar(gt[:], gt[:], 1.0, None, OP.add),
                  reads=(gkey,), writes=(gkey,), dbg="gates 1+")
                P("dve", lambda E, gt=gt: E.reciprocal(gt[:], gt[:]),
                  reads=(gkey,), writes=(gkey,), dbg="gates recip")
                if DBG:
                    dbg_store(gkey, dbgg_h[g * 128:(g + 1) * 128, :], gt[:])

        def be_heads(g, plist, after_pair=None):
            mask = maskv if g < 2 else mask2
            mkey = "maskv" if g < 2 else "mask2"
            col = (g % 2) * 128

            def emit_score_pair(p):
                bb0 = (0, 1) if p % 2 == 0 else (6, 7)
                for s_ in range(3):
                    slot = (g - 2 + s_) % KR
                    for hi in range(2):
                        rowb = 64 * hi
                        sbank = bb0[hi]
                        P("pe", lambda E, p=p, rowb=rowb, sbank=sbank, s_=s_, slot=slot: E.matmul(
                            psum[sbank][:, 128 * s_:128 * (s_ + 1)],
                            kb[p][rowb:rowb + 64, 128 * slot:128 * slot + 128],
                            q_fm[p][rowb:rowb + 64, col:col + 128],
                            start=True, stop=True, tile_position=(rowb, 0)),
                          reads=(f"qfm{p}", f"kb{p}"), writes=(f"ps{sbank}",), dbg="score")

            def emit_softmax(h, sbank):
                P("dve", lambda E, sbank=sbank, mask=mask: E.tensor_tensor(
                    psum[sbank][:, 0:384], psum[sbank][:, 0:384], mask[:], OP.add),
                  reads=(f"ps{sbank}", mkey), writes=(f"ps{sbank}",), dbg="mask add")
                P("act", lambda E, sbank=sbank, h=h: E.activation(
                    PTs[h % 2][:], psum[sbank][:, 0:384], AF.Exp, scale=c0125, bias=cm24),
                  reads=(f"ps{sbank}", "cvec"), writes=(f"PTs{h%2}",), dbg="exp")

            def emit_pv(h):
                lbank = 4 + h // 8
                for s_ in range(3):
                    slot = (g - 2 + s_) % KR
                    P("pe", lambda E, h=h, s_=s_, slot=slot, lbank=lbank: E.matmul(
                        psum[lbank][:, (h % 8) * 64:(h % 8) * 64 + 64],
                        PTs[h % 2][:, 128 * s_:128 * (s_ + 1)],
                        vb[slot][:, h * 64:h * 64 + 64],
                        start=(s_ == 0), stop=(s_ == 2)),
                      reads=(f"PTs{h%2}", f"vb{slot}"), writes=(f"ps{lbank}",), dbg="PV")
                    P("pe", lambda E, h=h, s_=s_: E.matmul(
                        psum[2][:, h:h + 1], PTs[h % 2][:, 128 * s_:128 * (s_ + 1)],
                        c1, start=(s_ == 0), stop=(s_ == 2)),
                      reads=(f"PTs{h%2}", "cvec"), writes=("ps2",), dbg="Psum mm")

            for p in plist:
                a, b = 2 * p, 2 * p + 1
                emit_score_pair(p)
                if p >= 1:
                    emit_pv(a - 2)
                    emit_pv(b - 2)
                bb0 = (0, 1) if p % 2 == 0 else (6, 7)
                emit_softmax(a, bb0[0])
                emit_softmax(b, bb0[1])
                if after_pair is not None:
                    after_pair()
            if plist[-1] == NH // 2 - 1:
                emit_pv(NH - 2)
                emit_pv(NH - 1)
                P("act", lambda E: E.copy(sm[:, 40:56], psum[2][:, 0:16]),
                  reads=("ps2",), writes=("smsums",), dbg="smsums evac")

        def be_mix(g, ktffn_cb=None, yffn_cb=None):
            gkey = f"gT{g % 4}"
            gt = gT[g % 4]
            col = (g % 2) * 128
            P("dve", lambda E: E.reciprocal(sm[:, 56:72], sm[:, 40:56]),
              reads=("smsums",), writes=("smfac",), dbg="recip")
            P("dve", lambda E, gt=gt: E.tensor_mul(
                sm[:, 56:72], sm[:, 56:72], bass.AP(gt, 0, [[32, 128], [2, 16]])),
              reads=("smfac", gkey), writes=("smfac",), dbg="gl*recip")
            for half in range(2):
                lb = 4 + half
                glview = bass.AP(sm, 56 + 8 * half, [[SMW, 128], [1, 8], [0, 64]])
                P("dve", lambda E, lb=lb, half=half, glview=glview: E.tensor_tensor(
                    mix_tm[:, 512 * half:512 * (half + 1)], psum[lb][:, :], glview, OP.mult),
                  reads=(f"ps{lb}", "smfac"), writes=("mixtm",), dbg="mix gl")
            for h in range(LH):
                c = h // 2
                rowb = (h % 2) * 64
                bank = h % 2
                r = h // 2
                P("pe", lambda E, c=c, rowb=rowb, bank=bank, r=r: E.matmul(
                    psum[bank][:, 64 * r:64 * (r + 1)],
                    gq[c][rowb:rowb + 64, col:col + 128],
                    gk[c][rowb:rowb + 64, :],
                    start=(r == 0), stop=(r == 3), tile_position=(rowb, 0)),
                  reads=(f"gq{c}", f"gk{c}"), writes=(f"ps{bank}",), dbg="lsc")
            for par in range(2):
                bank = par
                P("dve", lambda E, bank=bank, g=g: E.tensor_tensor(
                    psum[bank][:, 0:256], psum[bank][:, 0:256],
                    bass.AP(lms, 64 * g, [[1024, 128], [0, 4], [1, 64]]), OP.add),
                  reads=(f"ps{bank}", "lms"), writes=(f"ps{bank}",), dbg="lmask add")
                P("act", lambda E, bank=bank, par=par: E.activation(
                    bass.AP(lp, 64 * par, [[512, 128], [128, 4], [1, 64]]),
                    psum[bank][:, 0:256], AF.Exp, scale=c0125, bias=cm12),
                  reads=(f"ps{bank}", "cvec"), writes=("lp",), dbg="lexp")
            if ktffn_cb is not None:
                ktffn_cb()
            P("dve", lambda E: E.reduce_sum(sm[:, 8:16], lp[:].rearrange("p (h n) -> p h n", n=64),
                                            axis=mybir.AxisListType.X),
              reads=("lp",), writes=("sm8",), dbg="lsum")
            P("dve", lambda E: E.reciprocal(sm[:, 8:16], sm[:, 8:16]),
              reads=("sm8",), writes=("sm8",), dbg="lrecip")
            P("dve", lambda E: E.tensor_tensor(
                lp[:], lp[:], bass.AP(sm, 8, [[SMW, 128], [1, 8], [0, 64]]), OP.mult),
              reads=("lp", "sm8"), writes=("lp",), dbg="lp scale")
            def lpT_ap(cc):
                if cc < 2:
                    return lpT[cc][:], f"lpT{cc}"
                return PTs[cc - 2][:, 256:384], f"PTs{cc - 2}"
            for cc in range(4):
                bank = 2 + cc % 2
                P("pe", lambda E, cc=cc, bank=bank: E.matmul(
                    psum[bank][:, 0:128], lp[:, 128 * cc:128 * (cc + 1)], ident[:],
                    is_transpose=True, start=True, stop=True),
                  reads=("lp", "ident"), writes=(f"ps{bank}",), dbg="lpT")
                ap_, key_ = lpT_ap(cc)
                P("act", lambda E, ap_=ap_, bank=bank: E.copy(ap_, psum[bank][:, 0:128]),
                  reads=(f"ps{bank}",), writes=(key_,), dbg="lpT evac")
            for cc in range(4):
                ap_, key_ = lpT_ap(cc)
                P("pe", lambda E, ap_=ap_, cc=cc: E.matmul(
                    psum[0][0:64, 128 * cc:128 * cc + 128],
                    gvd[0:64, 128 * cc:128 * cc + 64], ap_[0:64, :],
                    start=True, stop=True),
                  reads=("gvd", key_), writes=("ps0",), dbg="rem A")
                P("pe", lambda E, ap_=ap_, cc=cc: E.matmul(
                    psum[0][64:128, 128 * cc:128 * cc + 128],
                    gvd[64:128, 128 * cc + 64:128 * cc + 128], ap_[64:128, :],
                    start=True, stop=True, tile_position=(64, 64)),
                  reads=("gvd", key_), writes=("ps0",), dbg="rem B")
            for ph in range(2):
                P("act", lambda E, ph=ph: E.copy(rem_fm[:, 256 * ph:256 * (ph + 1)],
                                                 psum[0][:, 256 * ph:256 * (ph + 1)]),
                  reads=("ps0",), writes=("remfm", f"remfm{ph}"), dbg="rem evac")
            for k in range(4):
                for half in range(2):
                    b = 2 + half
                    P("pe", lambda E, k=k, half=half, b=b: E.matmul(
                        psum[b][:, :], rem_fm[:, 128 * k:128 * (k + 1)],
                        rout_s[k][:, 512 * half:512 * (half + 1)],
                        start=(k == 0), stop=(k == 3)),
                      reads=(f"remfm{k//2}", f"routs{k}"), writes=(f"ps{b}",), dbg="remrout")
            if yffn_cb is not None:
                yffn_cb()
            for half in range(2):
                b = 2 + half
                grview = bass.AP(gt, 1 + 16 * half, [[32, 128], [2, 8], [0, 64]])
                P("dve", lambda E, b=b, half=half, grview=grview: E.tensor_tensor(
                    qD[:, 512 * half:512 * (half + 1)], psum[b][:, :], grview, OP.mult),
                  reads=(f"ps{b}", gkey), writes=QD, dbg="mix gr")
                P("dve", lambda E, half=half: E.tensor_tensor(
                    mix_tm[:, 512 * half:512 * (half + 1)],
                    mix_tm[:, 512 * half:512 * (half + 1)],
                    qD[:, 512 * half:512 * (half + 1)], OP.add),
                  reads=("mixtm",) + QD, writes=("mixtm",), dbg="mix add")
            if DBG:
                dbg_store("mixtm", dbgtm_h[3 * S + g * 128:3 * S + (g + 1) * 128, :], mix_tm[:])
        def be_wout(g):
            for half in range(2):
                bank = 2 + half
                for r in range(4):
                    c = half * 4 + r
                    P("pe", lambda E, c=c, bank=bank, r=r: E.matmul(
                        psum[bank][:, r * 128:(r + 1) * 128], mix_tm[:, 128 * c:128 * (c + 1)],
                        ident[:], is_transpose=True, start=(r == 0), stop=(r == 3)),
                      reads=("mixtm", "ident"), writes=(f"ps{bank}",), dbg="mixT")

                def mevac(E, half=half, bank=bank):
                    last = None
                    for r in range(4):
                        c = half * 4 + r
                        last = E.copy(mix_fm[:, 128 * c:128 * (c + 1)],
                                      psum[bank][:, r * 128:(r + 1) * 128])
                    return last
                P("act", mevac, reads=(f"ps{bank}",), writes=("mixfm", f"mixfm{half}"),
                  dbg="mixT evac")
            for k in range(8):
                for half in range(2):
                    b = 4 + half
                    P("pe", lambda E, k=k, half=half, b=b: E.matmul(
                        psum[b][:, :], mix_fm[:, 128 * k:128 * (k + 1)],
                        wout_s[k][:, 512 * half:512 * (half + 1)],
                        start=(k == 0), stop=(k == 7)),
                      reads=(f"mixfm{k//4}", f"wouts{k}"), writes=(f"ps{b}",), dbg="wout")

        n_stores = [0]

        def epi_evac(g):
            for half in range(2):
                b = 4 + half
                P("act", lambda E, b=b, half=half: E.copy(
                    attnout[:, 512 * half:512 * (half + 1)], psum[b][:, :]),
                  reads=(f"ps{b}",), writes=("mixfm", f"mixfm{half}"), dbg="attnout evac")
                P("act", lambda E, b=b, half=half: E.activation(
                    qA[:, 512 * half:512 * (half + 1)], psum[b][:, :], AF.Square),
                  reads=(f"ps{b}",), writes=(f"qA{half}",), dbg="ln2 sq")
            if DBG:
                dbg_store("mixfm", dbgtm_h[S + g * 128:S + (g + 1) * 128, :], attnout[:])

        def epi_segs(g):
            segs = []

            def Sg(eng, fn, **kw):
                segs.append(lambda eng=eng, fn=fn, kw=kw: P(eng, fn, **kw))

            xr = x_ring[g % XR]
            xkey = f"xr{g % XR}"
            Sg("dve", lambda E: E.reduce_sum(smc(24), attnout[:], axis=mybir.AxisListType.X),
               reads=("mixfm",), writes=("sm24",), dbg="ln2 s1")
            Sg("dve", lambda E: E.reduce_sum(smc(25), qA[:], axis=mybir.AxisListType.X),
               reads=QA, writes=("sm25",), dbg="ln2 s2")
            Sg("dve", lambda E: E.tensor_scalar(smc(24), smc(24), 1.0 / D, None, OP.mult),
               reads=("sm24",), writes=("sm24",), dbg="ln2 m")
            Sg("dve", lambda E: E.scalar_tensor_tensor(smc(26), smc(24), 1.0, smc(24), OP.mult, OP.mult),
               reads=("sm24",), writes=("sm26",), dbg="ln2 m2")
            Sg("dve", lambda E: E.scalar_tensor_tensor(smc(25), smc(25), 1.0 / D, smc(26), OP.mult, OP.subtract),
               reads=("sm25", "sm26"), writes=("sm25",), dbg="ln2 var")
            Sg("dve", lambda E: E.tensor_scalar(smc(25), smc(25), 1.0e-5, None, OP.add),
               reads=("sm25",), writes=("sm25",), dbg="ln2 +eps")
            Sg("act", lambda E: E.activation(smc(26), smc(25), AF.Ln),
               reads=("sm25",), writes=("sm26",), dbg="ln2 ln")
            Sg("act", lambda E: E.activation(smc(26), smc(26), AF.Exp, scale=cmhalf),
               reads=("sm26", "cvec"), writes=("sm26",), dbg="ln2 rsqrt")
            Sg("dve", lambda E: E.scalar_tensor_tensor(smc(27), smc(24), -1.0, smc(26), OP.mult, OP.mult),
               reads=("sm24", "sm26"), writes=("sm27",), dbg="ln2 bb")
            Sg("dve", lambda E: E.tensor_scalar(qD[:], attnout[:], smc(26), smc(27), OP.mult, OP.add),
               reads=("mixfm", "sm26", "sm27"), writes=QD, dbg="ln2 apply")
            Sg("dve", lambda E, xr=xr: E.tensor_add(y1[:], qD[:], xr[:]),
               reads=QD + (xkey,), writes=("y1",), dbg="y1")
            y1_pos = len(segs)
            if DBG:
                segs.append(lambda g=g: dbg_store(
                    "y1", dbgtm_h[2 * S + g * 128:2 * S + (g + 1) * 128, :], y1[:]))
            Sg("dve", lambda E: E.reduce_sum(smc(28), y1[:], axis=mybir.AxisListType.X),
               reads=("y1",), writes=("sm28",), dbg="ln3 s1")
            Sg("dve", lambda E: E.tensor_scalar(smc(28), smc(28), 1.0 / D, None, OP.mult),
               reads=("sm28",), writes=("sm28",), dbg="ln3 m")
            Sg("dve", lambda E: E.tensor_scalar(qD[:], y1[:], smc(28), None, OP.subtract),
               reads=("y1", "sm28"), writes=QD, dbg="u")
            Sg("dve", lambda E: E.reduce_max(smc(31), qD[:], axis=mybir.AxisListType.X,
                                             apply_absolute_value=True),
               reads=QD, writes=("sm31",), dbg="amax_u")
            Sg("dve", lambda E: E.reciprocal(smc(32), smc(31)),
               reads=("sm31",), writes=("sm32",), dbg="amax recip")
            for half in range(2):
                hs = slice(512 * half, 512 * (half + 1))
                Sg("dve", lambda E, hs=hs: E.tensor_scalar(qA[:, hs], qD[:, hs], smc(32), 127.0,
                                                           OP.mult, OP.mult),
                   reads=(QD[half], "sm32"), writes=(QA[half],), dbg="m=u*s")
                Sg("gps", lambda E, hs=hs: E.tensor_copy(qI[:, hs], qA[:, hs]),
                   reads=(QA[half],), writes=(QI[half],), dbg="k0 rint")
            for half in range(2):
                hs = slice(512 * half, 512 * (half + 1))
                Sg("dve", lambda E, hs=hs: E.tensor_copy(qD[:, hs], qI[:, hs]),
                   reads=(QI[half],), writes=(QD[half],), dbg="k0f")
                Sg("dve", lambda E, hs=hs: E.tensor_sub(qA[:, hs], qA[:, hs], qD[:, hs]),
                   reads=(QA[half], QD[half]), writes=(QA[half],), dbg="d")
                Sg("dve", lambda E, hs=hs: E.scalar_tensor_tensor(qD[:, hs], qA[:, hs], 0.5,
                                                                  qD[:, hs], OP.is_gt, OP.add),
                   reads=(QA[half], QD[half]), writes=(QD[half],), dbg="k1")
                Sg("dve", lambda E, hs=hs: E.tensor_scalar(qA[:, hs], qA[:, hs], -0.5, None, OP.is_lt),
                   reads=(QA[half],), writes=(QA[half],), dbg="e2")
                Sg("dve", lambda E, hs=hs: E.tensor_sub(qD[:, hs], qD[:, hs], qA[:, hs]),
                   reads=(QD[half], QA[half]), writes=(QD[half],), dbg="k final")
            Sg("act", lambda E: E.activation(mix_tm[:], y1[:], AF.Square),
               reads=("y1",), writes=("mixtm",), dbg="ln3 sq")
            Sg("dve", lambda E: E.reduce_sum(smc(29), mix_tm[:], axis=mybir.AxisListType.X),
               reads=("mixtm",), writes=("sm29",), dbg="ln3 s2")
            Sg("dve", lambda E: E.scalar_tensor_tensor(smc(30), smc(28), 1.0, smc(28), OP.mult, OP.mult),
               reads=("sm28",), writes=("sm30",), dbg="ln3 m2")
            Sg("dve", lambda E: E.scalar_tensor_tensor(smc(29), smc(29), 1.0 / D, smc(30), OP.mult, OP.subtract),
               reads=("sm29", "sm30"), writes=("sm29",), dbg="ln3 var")
            Sg("dve", lambda E: E.tensor_scalar(smc(29), smc(29), 1.0e-5, None, OP.add),
               reads=("sm29",), writes=("sm29",), dbg="ln3 +eps")
            Sg("act", lambda E: E.activation(smc(30), smc(29), AF.Ln),
               reads=("sm29",), writes=("sm30",), dbg="ln3 ln")
            Sg("act", lambda E: E.activation(smc(30), smc(30), AF.Exp, scale=cmhalf),
               reads=("sm30", "cvec"), writes=("sm30",), dbg="ln3 rsqrt")
            Sg("dve", lambda E: E.tensor_mul(smc(33), smc(31), smc(30)),
               reads=("sm31", "sm30"), writes=("sm33",), dbg="amax_h2")
            Sg("dve", lambda E: E.tensor_scalar(smc(33), smc(33), 1.0e-5, None, OP.max),
               reads=("sm33",), writes=("sm33",), dbg="amax clamp")
            Sg("dve", lambda E: E.tensor_scalar(smc(33), smc(33), ws_col[:], 1.0 / 127.0,
                                                OP.mult, OP.mult),
               reads=("sm33", "wscol"), writes=("sm33",), dbg="fscale")
            return segs, y1_pos

        def epi_ktffn(g):
            for half in range(2):
                bank = 2 + half
                for r in range(4):
                    c = half * 4 + r
                    P("pe", lambda E, c=c, bank=bank, r=r: E.matmul(
                        psum[bank][:, r * 128:(r + 1) * 128], qD[:, 128 * c:128 * (c + 1)],
                        ident[:], is_transpose=True, start=(r == 0), stop=(r == 3)),
                      reads=(QD[half], "ident"), writes=(f"ps{bank}",), dbg="kT")

                def kevac(E, half=half, bank=bank):
                    last = None
                    for r in range(4):
                        c = half * 4 + r
                        last = E.copy(kfm[c][:], psum[bank][:, r * 128:(r + 1) * 128])
                    return last
                P("act", kevac, reads=(f"ps{bank}",),
                  writes=tuple(f"kfm{half*4+r}" for r in range(4)), dbg="kT evac")
            for k in range(8):
                for half in range(2):
                    b = 6 + half
                    P("pe", lambda E, k=k, half=half, b=b: E.matmul(
                        psum[b][:, :], kfm[k][:], tern_s[k][:, 512 * half:512 * (half + 1)],
                        start=(k == 0), stop=(k == 7)),
                      reads=(f"kfm{k}", f"terns{k}"), writes=(f"ps{b}",), dbg="ffn mm")

        def epi_yffn(g):
            for half in range(2):
                b = 6 + half
                P("dve", lambda E, b=b, half=half: E.scalar_tensor_tensor(
                    qA[:, 512 * half:512 * (half + 1)], psum[b][:, :], smc(33),
                    bias_bc[:, 512 * half:512 * (half + 1)], OP.mult, OP.add),
                  reads=(f"ps{b}", "sm33", "biasbc"), writes=(QA[half],), dbg="yffn")

        def epi_pe_mm(g):
            epi_ktffn(g)
            epi_yffn(g)

        def epi_ln4(g):
            P("dve", lambda E: E.reduce_sum(smc(34), qA[:], axis=mybir.AxisListType.X),
              reads=QA, writes=("sm34",), dbg="ln4 s1")
            P("act", lambda E: E.activation(qD[:], qA[:], AF.Square),
              reads=QA, writes=QD, dbg="ln4 sq")
            P("dve", lambda E: E.reduce_sum(smc(35), qD[:], axis=mybir.AxisListType.X),
              reads=QD, writes=("sm35",), dbg="ln4 s2")
            P("dve", lambda E: E.tensor_scalar(smc(34), smc(34), 1.0 / D, None, OP.mult),
              reads=("sm34",), writes=("sm34",), dbg="ln4 m")
            P("dve", lambda E: E.scalar_tensor_tensor(smc(36), smc(34), 1.0, smc(34), OP.mult, OP.mult),
              reads=("sm34",), writes=("sm36",), dbg="ln4 m2")
            P("dve", lambda E: E.scalar_tensor_tensor(smc(35), smc(35), 1.0 / D, smc(36), OP.mult, OP.subtract),
              reads=("sm35", "sm36"), writes=("sm35",), dbg="ln4 var")
            P("dve", lambda E: E.tensor_scalar(smc(35), smc(35), 1.0e-5, None, OP.add),
              reads=("sm35",), writes=("sm35",), dbg="ln4 +eps")
            P("act", lambda E: E.activation(smc(36), smc(35), AF.Ln),
              reads=("sm35",), writes=("sm36",), dbg="ln4 ln")
            P("act", lambda E: E.activation(smc(36), smc(36), AF.Exp, scale=cmhalf),
              reads=("sm36", "cvec"), writes=("sm36",), dbg="ln4 rsqrt")
            P("dve", lambda E: E.scalar_tensor_tensor(smc(37), smc(34), -1.0, smc(36), OP.mult, OP.mult),
              reads=("sm34", "sm36"), writes=("sm37",), dbg="ln4 bb")
            P("dve", lambda E: E.tensor_scalar(qD[:], qA[:], smc(36), smc(37), OP.mult, OP.add),
              reads=QA + ("sm36", "sm37"), writes=QD, dbg="ln4 apply")

        def epi_y2(g):
            P("dve", lambda E: E.tensor_add(attnout[:], qD[:], y1[:]),
              reads=QD + ("y1",), writes=("mixfm", "mixfm0", "mixfm1"), dbg="y2")
            n_stores[0] += 1
            dma_store("mixfm", y_hbm[g * 128:(g + 1) * 128, :], attnout[:])

        pend = {"segs": [], "y1pos": 0, "done": 0}

        def drain(n):
            s = pend["segs"]
            stop = min(pend["done"] + n, len(s))
            for i in range(pend["done"], stop):
                s[i]()
            pend["done"] = stop

        if SCHED == "seq":
            for b0 in range(0, NQT, 2):
                fe_A(b0)
                fe_A(b0 + 1)
                fe_Qpair(b0)
                fe_KVpair(b0)
                fe_GQgpair(b0)
                for g in (b0, b0 + 1):
                    be_heads(g, list(range(0, 4)))
                    be_heads(g, list(range(4, 8)))
                    if g == 0:
                        dma_load("maskv", maskv[:], m1_h[:, :])
                    be_mix(g)
                    be_wout(g)
                    epi_evac(g)
                    pend["segs"], pend["y1pos"] = epi_segs(g)
                    pend["done"] = 0
                    drain(10**9)
                    epi_pe_mm(g)
                    epi_ln4(g)
                    epi_y2(g)
        else:
            fe_A(0)
            fe_A(1)
            fe_Qpair(0)
            fe_KVpair(0)
            fe_GQgpair(0)
            dr5 = lambda: drain(4)
            for g in range(NQT):
                be_heads(g, list(range(0, 2)), after_pair=dr5)
                if g + 2 < NQT:
                    fe_A_ln(g + 2)
                be_heads(g, list(range(2, 4)), after_pair=dr5)
                be_heads(g, list(range(4, 8)), after_pair=dr5)
                if g == 0:
                    dma_load("maskv", maskv[:], m1_h[:, :])
                drain(10**9)
                if g + 2 < NQT:
                    fe_A_hT(g + 2)

                def fill(g=g):
                    if g % 2 == 1 and g + 2 < NQT:
                        fe_Qpair(g + 1)
                    if g >= 1:
                        epi_ktffn(g - 1)
                ycb = (lambda gg=g - 1: epi_yffn(gg)) if g >= 1 else None
                be_mix(g, ktffn_cb=fill, yffn_cb=ycb)
                if g >= 1:
                    epi_ln4(g - 1)
                be_wout(g)
                if g >= 1:
                    epi_y2(g - 1)
                epi_evac(g)
                if g % 2 == 1 and g + 2 < NQT:
                    fe_KVpair(g + 1)
                    fe_GQgpair(g + 1)
                pend["segs"], pend["y1pos"] = epi_segs(g)
                pend["done"] = 0
            drain(10**9)
            epi_pe_mm(NQT - 1)
            epi_ln4(NQT - 1)
            epi_y2(NQT - 1)

        eng_counts = {e: 0 for e in ("pe", "dve", "act", "gps")}
        stream_counts = {}
        stream_eng = {}
        needed = set()
        for n in plan.nodes:
            for d in n.waits.values():
                needed.add(id(d))
        for n in plan.nodes:
            if n.is_dma:
                c = stream_counts.get(n.stream, 0) + 16
                stream_counts[n.stream] = c
                stream_eng[n.stream] = n.eng
                n.inc_val = (n.stream, c)
            elif id(n) in needed:
                eng_counts[n.eng] += 1
                n.inc_val = (n.eng, eng_counts[n.eng])

        with nc.Block() as block:
            def emit_engine(eng, E):
                waited = {}
                for n in plan.by_eng[eng]:
                    for d in n.waits.values():
                        iv = d.inc_val
                        assert iv is not None, f"dep of {n.dbg} on {d.dbg} has no inc"
                        semname, val = iv
                        if waited.get(semname, 0) >= val:
                            continue
                        waited[semname] = val
                        E.wait_ge(sems[semname], val)
                    instr = n.fn(E)
                    if n.inc_val is not None:
                        semname, val = n.inc_val
                        instr.then_inc(sems[semname], 16 if n.is_dma else 1)
                for s_, cnt in stream_counts.items():
                    if stream_eng[s_] == eng:
                        E.wait_ge(sems[s_], cnt)

            @block.tensor
            def _(E):
                emit_engine("pe", E)

            @block.vector
            def _(E):
                emit_engine("dve", E)

            @block.scalar
            def _(E):
                emit_engine("act", E)

            @block.gpsimd
            def _(E):
                emit_engine("gps", E)

            @block.sync
            def _(E):
                emit_engine("sp", E)

    return nc


def _lmask_table():
    import ml_dtypes
    out = np.empty((2048, 64), dtype=ml_dtypes.bfloat16)
    for g in range(16):
        r = np.arange(128)[:, None]
        n = np.arange(64)[None, :]
        out[128 * g:128 * (g + 1)] = np.where(n <= 4 * g + r // 32, 0.0, NEG)
    return out


def _host_constants():
    import ml_dtypes
    ident = np.eye(128, dtype=np.float32)
    r = np.arange(128)[:, None]
    j = np.arange(384)[None, :]
    base = (j - r >= 1) & (j - r <= 256)
    m2 = np.where(base, 0.0, NEG).astype(ml_dtypes.bfloat16)
    m1 = np.where(base & (j >= 128), 0.0, NEG).astype(ml_dtypes.bfloat16)
    m0 = np.where(base & (j >= 256), 0.0, NEG).astype(ml_dtypes.bfloat16)

    def tr(m):
        out = np.empty_like(m)
        for s in range(3):
            out[:, 128 * s:128 * (s + 1)] = m[:, 128 * s:128 * (s + 1)].T
        return out
    return ident, tr(m0), tr(m1), tr(m2)


def kernel(L0_attn_x, L0_attn_wq, L0_attn_wk, L0_attn_wv, L0_attn_wgq, L0_attn_wgk,
           L0_attn_wgv, L0_attn_r_out_w, L0_attn_gate_local_w, L0_attn_gate_remote_w,
           L0_attn_w_out, L0_ffn_packed_weight, L0_ffn_weight_scale, L0_ffn_bias):
    x = np.ascontiguousarray(np.asarray(L0_attn_x, dtype=np.float32))
    ident, m0, m1, m2 = _host_constants()
    glw = np.asarray(L0_attn_gate_local_w, dtype=np.float32)
    grw = np.asarray(L0_attn_gate_remote_w, dtype=np.float32)
    gwbd = np.zeros((D, 32), dtype=np.float32)
    for h in range(NH):
        gwbd[64 * h:64 * (h + 1), 2 * h] = glw[h]
        gwbd[64 * h:64 * (h + 1), 2 * h + 1] = grw[h]
    common = dict(
        wq=np.ascontiguousarray(L0_attn_wq, dtype=np.float32),
        wk=np.ascontiguousarray(L0_attn_wk, dtype=np.float32),
        wv=np.ascontiguousarray(L0_attn_wv, dtype=np.float32),
        wgq=np.ascontiguousarray(L0_attn_wgq, dtype=np.float32),
        wgk=np.ascontiguousarray(L0_attn_wgk, dtype=np.float32),
        wgv=np.ascontiguousarray(L0_attn_wgv, dtype=np.float32),
        rout=np.ascontiguousarray(L0_attn_r_out_w, dtype=np.float32),
        wout=np.ascontiguousarray(L0_attn_w_out, dtype=np.float32),
        gwbd=gwbd,
        packed=np.ascontiguousarray(L0_ffn_packed_weight, dtype=np.int32),
        wscale=np.ascontiguousarray(L0_ffn_weight_scale, dtype=np.float32),
        fbias=np.ascontiguousarray(L0_ffn_bias, dtype=np.float32),
        ident=ident, m0=m0, m1=m1, m2=m2,
        zeros=np.zeros((128, 1024), dtype=np.float32),
        cvec=np.tile(np.array([0.125, -24.0, -12.0, -0.5, 0.0, -1.0, 1.0, 0.0],
                              dtype=np.float32), (128, 1)),
        lmasks=_lmask_table(),
    )
    in_maps = [dict(common, x=np.ascontiguousarray(x[b])) for b in range(8)]
    nc = build_program()
    res = run_bass_kernel_spmd(nc, in_maps, core_ids=list(range(8)))
    out = np.stack([np.asarray(r["y"], dtype=np.float32) for r in res.results], axis=0)
    return out


# version 30
# speedup vs baseline: 1.0026x; 1.0026x over previous
import contextlib
import numpy as np

import concourse.bass as bass
import concourse.mybir as mybir
from concourse.bass_utils import run_bass_kernel_spmd
from concourse.alu_op_type import AluOpType as OP

F32 = mybir.dt.float32
BF16 = mybir.dt.bfloat16
I32 = mybir.dt.int32
AF = mybir.ActivationFunctionType

S = 2048
D = 1024
NH = 16
HD = 64
LH = 8
LD = 512
NLAT = 64
CH = 32
NQT = 16
NEG = -30000.0
KR = 4
XR = 4
RW = 4
RG = 3

DBG = False
SCHED = "il"
ENGINES = ("pe", "dve", "act", "gps", "sp")
QA = ("qA0", "qA1")
QD = ("qD0", "qD1")
QI = ("qI0", "qI1")


class Node:
    __slots__ = ("eng", "fn", "waits", "inc_val", "is_dma", "seq", "dbg", "eng_pos",
                 "stream")

    def __init__(self, eng, fn, seq, dbg):
        self.eng = eng
        self.fn = fn
        self.waits = {}
        self.inc_val = None
        self.is_dma = False
        self.seq = seq
        self.dbg = dbg
        self.eng_pos = 0
        self.stream = None


class Plan:
    GAP = 10**9

    def __init__(self):
        self.nodes = []
        self.by_eng = {e: [] for e in ENGINES}
        self.last_w = {}
        self.readers = {}
        self.stream_last = {}

    def add(self, eng, fn, reads=(), writes=(), is_dma=False, dbg=""):
        n = Node(eng, fn, len(self.nodes), dbg)
        n.is_dma = is_dma
        n.eng_pos = len(self.by_eng[eng])
        deps = []
        for b in reads:
            w = self.last_w.get(b)
            if w is not None:
                deps.append(w)
        for b in writes:
            w = self.last_w.get(b)
            if w is not None:
                deps.append(w)
            deps.extend(self.readers.get(b, ()))
        for d in deps:
            if d is n:
                continue
            if d.eng == eng and not d.is_dma and n.eng_pos - d.eng_pos > self.GAP:
                continue
            if id(d) not in n.waits:
                n.waits[id(d)] = d
        for b in reads:
            self.readers.setdefault(b, []).append(n)
        for b in writes:
            self.last_w[b] = n
            self.readers[b] = []
        self.nodes.append(n)
        self.by_eng[eng].append(n)
        return n


def build_program():
    nc = bass.Bass("TRN2", target_bir_lowering=False, detect_race_conditions=False)

    x_hbm = nc.dram_tensor("x", [S, D], F32, kind="ExternalInput").ap()
    wq_h = nc.dram_tensor("wq", [D, D], F32, kind="ExternalInput").ap()
    wk_h = nc.dram_tensor("wk", [D, D], F32, kind="ExternalInput").ap()
    wv_h = nc.dram_tensor("wv", [D, D], F32, kind="ExternalInput").ap()
    wgq_h = nc.dram_tensor("wgq", [D, LD], F32, kind="ExternalInput").ap()
    wgk_h = nc.dram_tensor("wgk", [D, LD], F32, kind="ExternalInput").ap()
    wgv_h = nc.dram_tensor("wgv", [D, LD], F32, kind="ExternalInput").ap()
    rout_h = nc.dram_tensor("rout", [LD, D], F32, kind="ExternalInput").ap()
    wout_h = nc.dram_tensor("wout", [D, D], F32, kind="ExternalInput").ap()
    gwbd_h = nc.dram_tensor("gwbd", [D, 32], F32, kind="ExternalInput").ap()
    packed_h = nc.dram_tensor("packed", [262144], I32, kind="ExternalInput").ap()
    ws_h = nc.dram_tensor("wscale", [1], F32, kind="ExternalInput").ap()
    bias_h = nc.dram_tensor("fbias", [D], F32, kind="ExternalInput").ap()
    ident_h = nc.dram_tensor("ident", [128, 128], F32, kind="ExternalInput").ap()
    m0_h = nc.dram_tensor("m0", [128, 384], BF16, kind="ExternalInput").ap()
    m1_h = nc.dram_tensor("m1", [128, 384], BF16, kind="ExternalInput").ap()
    m2_h = nc.dram_tensor("m2", [128, 384], BF16, kind="ExternalInput").ap()
    zeros_h = nc.dram_tensor("zeros", [128, 1024], F32, kind="ExternalInput").ap()
    cvec_h = nc.dram_tensor("cvec", [128, 8], F32, kind="ExternalInput").ap()
    lmasks_h = nc.dram_tensor("lmasks", [2048, 64], BF16, kind="ExternalInput").ap()
    y_hbm = nc.dram_tensor("y", [S, D], F32, kind="ExternalOutput").ap()
    if DBG:
        dbgfm_h = nc.dram_tensor("dbgfm", [3 * 16384, 128], F32, kind="ExternalOutput").ap()
        dbgtm_h = nc.dram_tensor("dbgtm", [4 * S, D], F32, kind="ExternalOutput").ap()
        dbgg_h = nc.dram_tensor("dbgg", [S, 32], F32, kind="ExternalOutput").ap()

    es = contextlib.ExitStack()
    sb = lambda name, shape, dt=F32: es.enter_context(nc.sbuf_tensor(name, shape, dt))
    ps = lambda name: es.enter_context(nc.psum_tensor(name, [128, 512], F32))

    with es:
        h_fm = [sb(f"hfm{c}", [128, 256]) for c in range(8)]
        q_fm = [sb(f"qfm{c}", [128, 256]) for c in range(8)]
        gq = [sb(f"gq{c}", [128, 256]) for c in range(4)]
        kb = [sb(f"kb{c}", [128, 128 * KR]) for c in range(8)]
        vb = [sb(f"vb{s}", [128, D]) for s in range(KR)]
        wout_s = [sb(f"wouts{k}", [128, D]) for k in range(8)]
        rout_s = [sb(f"routs{k}", [128, D]) for k in range(4)]
        tern_s = [sb(f"terns{k}", [128, D], BF16) for k in range(8)]
        ring_w = [sb(f"ringw{i}", [128, D]) for i in range(RW)]
        ring_g = [sb(f"ringg{i}", [128, LD]) for i in range(RG)]
        pooled = [sb(f"pool{c}", [128, NLAT]) for c in range(8)]
        gk = [sb(f"gk{c}", [128, NLAT]) for c in range(4)]
        gvd = sb("gvd", [128, LD])
        gw_s = sb("gws", [128, 256])
        maskv = sb("maskv", [128, 384], BF16)
        mask2 = sb("mask2t", [128, 384], BF16)
        lms = sb("lms", [128, 1024], BF16)
        ident = sb("identt", [128, 128])
        bias_bc = sb("biasbc", [128, D])
        ws_col = sb("wscol", [128, 1])
        gT = [sb(f"gT{i}", [128, 32]) for i in range(4)]
        x_ring = [sb(f"xr{i}", [128, D]) for i in range(XR)]
        h_tm = sb("htm", [128, D])
        PTs = [sb(f"PTs{i}", [128, 384]) for i in range(2)]
        lp = sb("lp", [128, 512])
        lpT3 = [sb(f"lpT{c}", [128, 128]) for c in range(2)]
        lpT = lpT3 + [None, None]
        rem_fm = sb("remfm", [128, 512])
        mix_tm = sb("mixtm", [128, D])
        mix_fm = sb("mixfm", [128, D])
        attnout = mix_fm
        y1 = sb("y1", [128, D])
        qA = sb("qA", [128, D])
        qD = sb("qD", [128, D])
        qI = sb("qI", [128, D], I32)
        kfm = [sb(f"kfm{k}", [128, 128], BF16) for k in range(8)]
        tp = qI[:]
        tfw = h_tm[:]
        tern_om = qA
        SMW = 72
        sm = sb("small", [128, SMW])
        smc = lambda i: sm[:, i:i + 1]
        cvec = sb("cvec_s", [128, 8])
        c0125 = cvec[:, 0:1]
        cm24 = cvec[:, 1:2]
        cm12 = cvec[:, 2:3]
        cmhalf = cvec[:, 3:4]
        cm1 = cvec[:, 5:6]
        c1 = cvec[:, 6:7]

        psum = [ps(f"pp{i}") for i in range(8)]

        STREAMS = tuple(f"rw{i}" for i in range(RW)) + \
            tuple(f"rg{i}" for i in range(RG)) + ("mi0", "mi1", "st", "db0", "db1")
        sems = {}
        for e in ("pe", "dve", "act", "gps"):
            sems[e] = es.enter_context(nc.semaphore(f"sem_{e}"))
        for s_ in STREAMS:
            sems[s_] = es.enter_context(nc.semaphore(f"sem_{s_}"))

        plan = Plan()
        P = plan.add
        mi_rr = [0]

        def dma_node(eng, stream, fn, reads, writes, dbg):
            n = P(eng, fn, reads=reads, writes=writes, is_dma=True, dbg=dbg)
            n.stream = stream
            prev = plan.stream_last.get(stream)
            if prev is not None:
                n.waits[id(prev)] = prev
            plan.stream_last[stream] = n
            return n

        def dma_load(buf_keys, out_ap, in_ap, extra_reads=(), stream=None):
            if isinstance(buf_keys, str):
                buf_keys = (buf_keys,)
            if stream is None:
                stream = f"mi{mi_rr[0] % 2}"
                mi_rr[0] += 1

            def f(E, out_ap=out_ap, in_ap=in_ap):
                return E.dma_start(out=out_ap, in_=in_ap)
            return dma_node("sp", stream, f, extra_reads, buf_keys,
                            f"load {buf_keys[0]}")

        db_rr = [0]

        def dma_store(in_key, out_ap, in_ap, stream="st"):
            def f(E, out_ap=out_ap, in_ap=in_ap):
                return E.dma_start(out=out_ap, in_=in_ap)
            return dma_node("gps", stream, f, (in_key,), (), f"store {in_key}")

        def dbg_store(in_key, out_ap, in_ap):
            s_ = f"db{db_rr[0] % 2}"
            db_rr[0] += 1
            return dma_store(in_key, out_ap, in_ap, stream=s_)

        dma_load("ident", ident[:], ident_h[:, :])
        dma_load("maskv", maskv[:], m0_h[:, :])
        dma_load("mask2", mask2[:], m2_h[:, :])
        for k in range(8):
            dma_load("gws", gw_s[:, 32 * k:32 * (k + 1)], gwbd_h[128 * k:128 * (k + 1), :])
        dma_load("biasbc", bias_bc[:], bass.AP(bias_h.tensor, 0, [[0, 128], [1, D]]))
        dma_load("wscol", ws_col[:], bass.AP(ws_h.tensor, 0, [[0, 128], [1, 1]]))
        dma_load("cvec", cvec[:], cvec_h[:, :])
        for g in range(16):
            dma_load("lms", lms[:, 64 * g:64 * (g + 1)], lmasks_h[128 * g:128 * (g + 1), :])
        for k in range(8):
            dma_load(f"wouts{k}", wout_s[k][:], wout_h[128 * k:128 * (k + 1), :])
        for k in range(4):
            dma_load(f"routs{k}", rout_s[k][:], rout_h[128 * k:128 * (k + 1), :])
        for c in range(8):
            dma_load(f"pool{c}", pooled[c][:], zeros_h[:, 0:NLAT])
        for c in range(4):
            dma_load(f"gk{c}", gk[c][:], zeros_h[:, 0:NLAT])
        dma_load("gvd", gvd[:], zeros_h[:, 0:LD])
        for c in range(8):
            for s_ in (2, 3):
                dma_load(f"kb{c}", kb[c][:, 128 * s_:128 * (s_ + 1)], zeros_h[:, 0:128])
        for s_ in (2, 3):
            dma_load(f"vb{s_}", vb[s_][:], zeros_h[:, :])

        for d in range(8):
            dma_load(("tp",) + QI, tp,
                     bass.AP(packed_h.tensor, 131072 * (d % 2), [[1024, 128], [1, 1024]]))
            sh = 2 * (d // 2)
            if sh:
                P("dve", lambda E, sh=sh: E.tensor_scalar(tp, tp, sh, None, OP.logical_shift_right),
                  reads=("tp",) + QI, writes=("tp",) + QI, dbg="unpack shift")
            P("dve", lambda E: E.tensor_scalar(tp, tp, 3, None, OP.bitwise_and),
              reads=("tp",) + QI, writes=("tp",) + QI, dbg="unpack and")
            P("dve", lambda E: E.tensor_copy(tfw, tp),
              reads=("tp",) + QI, writes=("tfw", "htm"), dbg="unpack i2f")
            P("dve", lambda E: E.tensor_scalar(tern_om[:], tfw, 1.0, None, OP.subtract),
              reads=("tfw", "htm"), writes=("tom",) + QA, dbg="unpack sub")
            for c in range(8):
                bank = 2 + c % 2
                P("pe", lambda E, c=c, d=d, bank=bank: E.matmul(
                    psum[bank][:, 0:128], tern_om[:, 128 * c:128 * (c + 1)],
                    ident[:], is_transpose=True, start=True, stop=True),
                  reads=("tom", "ident") + QA, writes=(f"ps{bank}",), dbg="ternT transpose")
                P("act", lambda E, c=c, d=d, bank=bank: E.copy(
                    tern_s[c][:, 128 * d:128 * (d + 1)], psum[bank][:, 0:128]),
                  reads=(f"ps{bank}",), writes=(f"terns{c}",), dbg="ternT evac")

        widx = [0]
        gidx = [0]

        def wload(src_ap):
            i = widx[0] % RW
            widx[0] += 1
            dma_load(f"ringw{i}", ring_w[i][:], src_ap, stream=f"rw{i}")
            return i, f"ringw{i}"

        def gload(src_ap):
            i = gidx[0] % RG
            gidx[0] += 1
            dma_load(f"ringg{i}", ring_g[i][:], src_ap, stream=f"rg{i}")
            return i, f"ringg{i}"

        def ln_rs_bb(src_key, src_ap, dump_keys, dump_ap, i_s1, i_s2, i_rs, i_bb):
            s1, s2, rs, bb = smc(i_s1), smc(i_s2), smc(i_rs), smc(i_bb)
            P("dve", lambda E: E.reduce_sum(s1, src_ap, axis=mybir.AxisListType.X),
              reads=(src_key,), writes=(f"sm{i_s1}",), dbg="ln s1")
            P("act", lambda E: E.activation(dump_ap, src_ap, AF.Square),
              reads=(src_key,), writes=dump_keys, dbg="ln sq")
            P("dve", lambda E: E.reduce_sum(s2, dump_ap, axis=mybir.AxisListType.X),
              reads=dump_keys, writes=(f"sm{i_s2}",), dbg="ln s2")
            P("dve", lambda E: E.tensor_scalar(s1, s1, 1.0 / D, None, OP.mult),
              reads=(f"sm{i_s1}",), writes=(f"sm{i_s1}",), dbg="ln m")
            P("dve", lambda E: E.scalar_tensor_tensor(rs, s1, 1.0, s1, OP.mult, OP.mult),
              reads=(f"sm{i_s1}",), writes=(f"sm{i_rs}",), dbg="ln m2")
            P("dve", lambda E: E.scalar_tensor_tensor(s2, s2, 1.0 / D, rs, OP.mult, OP.subtract),
              reads=(f"sm{i_s2}", f"sm{i_rs}"), writes=(f"sm{i_s2}",), dbg="ln var")
            P("dve", lambda E: E.tensor_scalar(s2, s2, 1.0e-5, None, OP.add),
              reads=(f"sm{i_s2}",), writes=(f"sm{i_s2}",), dbg="ln +eps")
            P("act", lambda E: E.activation(rs, s2, AF.Ln),
              reads=(f"sm{i_s2}",), writes=(f"sm{i_rs}",), dbg="ln ln")
            P("act", lambda E: E.activation(rs, rs, AF.Exp, scale=cmhalf),
              reads=(f"sm{i_rs}", "cvec"), writes=(f"sm{i_rs}",), dbg="ln rsqrt")
            P("dve", lambda E: E.scalar_tensor_tensor(bb, s1, -1.0, rs, OP.mult, OP.mult),
              reads=(f"sm{i_s1}", f"sm{i_rs}"), writes=(f"sm{i_bb}",), dbg="ln bb")
            return rs, bb

        def fe_A_ln(g):
            xr = x_ring[g % XR]
            xkey = f"xr{g % XR}"
            dma_load(xkey, xr[:], x_hbm[g * 128:(g + 1) * 128, :])
            rs, bb = ln_rs_bb(xkey, xr[:], ("htm", "tfw"), h_tm[:], 0, 1, 2, 3)
            P("dve", lambda E, xr=xr, rs=rs, bb=bb: E.tensor_scalar(
                h_tm[:], xr[:], rs, bb, OP.mult, OP.add),
              reads=(xkey, "sm2", "sm3"), writes=("htm", "tfw"), dbg="ln1 apply")

        def fe_A_hT(g):
            col = (g % 2) * 128
            for half in range(2):
                bank = 2 + half
                for r in range(4):
                    c = half * 4 + r
                    P("pe", lambda E, c=c, bank=bank, r=r: E.matmul(
                        psum[bank][:, r * 128:(r + 1) * 128], h_tm[:, 128 * c:128 * (c + 1)],
                        ident[:], is_transpose=True, start=(r == 0), stop=(r == 3)),
                      reads=("htm", "ident"), writes=(f"ps{bank}",), dbg="hT")

                def evac(E, half=half, bank=bank, col=col):
                    last = None
                    for r in range(4):
                        c = half * 4 + r
                        last = E.copy(h_fm[c][:, col:col + 128],
                                      psum[bank][:, r * 128:(r + 1) * 128])
                    return last
                P("act", evac, reads=(f"ps{bank}",),
                  writes=tuple(f"hfm{half*4+r}" for r in range(4)), dbg="hT evac")
            if DBG:
                for c in range(8):
                    r0 = (g * 8 + c) * 128
                    dbg_store(f"hfm{c}", dbgfm_h[r0:r0 + 128, :], h_fm[c][:, col:col + 128])
            for c in range(8):
                lat0 = 4 * g
                P("dve", lambda E, c=c, col=col, lat0=lat0: E.reduce_sum(
                    pooled[c][:, lat0:lat0 + 4],
                    h_fm[c][:, col:col + 128].rearrange("p (s q) -> p s q", q=32),
                    axis=mybir.AxisListType.X),
                  reads=(f"hfm{c}",), writes=(f"pool{c}",), dbg="pool reduce")
                P("dve", lambda E, c=c, lat0=lat0: E.tensor_scalar(
                    pooled[c][:, lat0:lat0 + 4], pooled[c][:, lat0:lat0 + 4],
                    1.0 / CH, None, OP.mult),
                  reads=(f"pool{c}",), writes=(f"pool{c}",), dbg="pool scale")
            if g % 2 == 1:
                lat0 = 4 * (g - 1)
                for k in range(8):
                    i, keyg = gload(wgk_h[128 * k:128 * (k + 1), :])
                    for m in range(4):
                        P("pe", lambda E, i=i, k=k, m=m, lat0=lat0: E.matmul(
                            psum[0][:, 8 * m:8 * m + 8], ring_g[i][:, 128 * m:128 * (m + 1)],
                            pooled[k][:, lat0:lat0 + 8],
                            start=(k == 0 and m == 0), stop=(k == 7 and m == 3)),
                          reads=(keyg, f"pool{k}"), writes=("ps0",), dbg="GK mm")
                for m in range(4):
                    P("act", lambda E, m=m, lat0=lat0: E.copy(gk[m][:, lat0:lat0 + 8],
                                                              psum[0][:, 8 * m:8 * m + 8]),
                      reads=("ps0",), writes=(f"gk{m}",), dbg="GK evac")
                for k in range(8):
                    i, keyg = gload(wgv_h[128 * k:128 * (k + 1), :])
                    P("pe", lambda E, i=i, k=k: E.matmul(
                        psum[1][0:64, :], pooled[k][:, :], ring_g[i][:, :],
                        start=(k == 0), stop=(k == 7)),
                      reads=(keyg, f"pool{k}"), writes=("ps1",), dbg="GV mm")
                P("act", lambda E: E.copy(gvd[0:64, :], psum[1][0:64, :]),
                  reads=("ps1",), writes=("gvd",), dbg="GV evac")
                dma_load("gvd", gvd[64:128, :], gvd[0:64, :], extra_reads=("gvd",))

        def fe_A(g):
            fe_A_ln(g)
            fe_A_hT(g)

        def proj_pair(w_h, out_cb, dbgname):
            for k in range(8):
                i, keyw = wload(w_h[128 * k:128 * (k + 1), :])
                for m in range(8):
                    bk = m % 4
                    reg = 256 * (m // 4)
                    P("pe", lambda E, i=i, k=k, m=m, bk=bk, reg=reg: E.matmul(
                        psum[bk][:, reg:reg + 256], ring_w[i][:, 128 * m:128 * (m + 1)],
                        h_fm[k][:, 0:256],
                        start=(k == 0 and m < 4), stop=(k == 7 and m >= 4)),
                      reads=(keyw, f"hfm{k}"), writes=(f"ps{bk}",), dbg=dbgname)
            for m in range(8):
                out_cb(m, m % 4, 256 * (m // 4))

        def fe_Qpair(b0):
            def out_cb(m, bk, reg):
                P("act", lambda E, m=m, bk=bk, reg=reg: E.copy(
                    q_fm[m][:, 0:256], psum[bk][:, reg:reg + 256]),
                  reads=(f"ps{bk}",), writes=(f"qfm{m}",), dbg="Q evac")
            proj_pair(wq_h, out_cb, "Q mm")
            if DBG:
                for qi in range(2):
                    for c in range(8):
                        r0 = 16384 + ((b0 + qi) * 8 + c) * 128
                        dbg_store(f"qfm{c}", dbgfm_h[r0:r0 + 128, :],
                                  q_fm[c][:, 128 * qi:128 * qi + 128])

        def fe_KVpair(b0):
            s0 = b0 % KR

            def out_cb(m, bk, reg):
                P("act", lambda E, m=m, bk=bk, reg=reg: E.copy(
                    kb[m][:, 128 * s0:128 * s0 + 256], psum[bk][:, reg:reg + 256]),
                  reads=(f"ps{bk}",), writes=(f"kb{m}",), dbg="K evac")
            proj_pair(wk_h, out_cb, "K mm")
            if DBG:
                for qi in range(2):
                    for c in range(8):
                        r0 = 2 * 16384 + ((b0 + qi) * 8 + c) * 128
                        dbg_store(f"kb{c}", dbgfm_h[r0:r0 + 128, :],
                                  kb[c][:, 128 * (s0 + qi):128 * (s0 + qi) + 128])
            for k in range(8):
                i, keyw = wload(wv_h[128 * k:128 * (k + 1), :])
                for qi in range(2):
                    for half in range(2):
                        bk = qi * 2 + half
                        P("pe", lambda E, i=i, k=k, qi=qi, half=half, bk=bk: E.matmul(
                            psum[bk][:, :], h_fm[k][:, 128 * qi:128 * (qi + 1)],
                            ring_w[i][:, 512 * half:512 * (half + 1)],
                            start=(k == 0), stop=(k == 7)),
                          reads=(keyw, f"hfm{k}"), writes=(f"ps{bk}",), dbg="V mm")
            for qi in range(2):
                slot = (b0 + qi) % KR
                for half in range(2):
                    bk = qi * 2 + half
                    P("act", lambda E, half=half, slot=slot, bk=bk: E.copy(
                        vb[slot][:, 512 * half:512 * (half + 1)], psum[bk][:, :]),
                      reads=(f"ps{bk}",), writes=(f"vb{slot}",), dbg="V evac")
                if DBG:
                    g = b0 + qi
                    dbg_store(f"vb{slot}", dbgtm_h[g * 128:(g + 1) * 128, :], vb[slot][:])

        def fe_GQgpair(b0):
            for k in range(8):
                i, keyg = gload(wgq_h[128 * k:128 * (k + 1), :])
                for m in range(4):
                    P("pe", lambda E, i=i, k=k, m=m: E.matmul(
                        psum[m][:, 0:256], ring_g[i][:, 128 * m:128 * (m + 1)],
                        h_fm[k][:, 0:256], start=(k == 0), stop=(k == 7)),
                      reads=(keyg, f"hfm{k}"), writes=(f"ps{m}",), dbg="GQ mm")
            for m in range(4):
                P("act", lambda E, m=m: E.copy(gq[m][:, 0:256], psum[m][:, 0:256]),
                  reads=(f"ps{m}",), writes=(f"gq{m}",), dbg="GQ evac")
            for qi in range(2):
                g = b0 + qi
                col = 128 * qi
                for k in range(8):
                    P("pe", lambda E, k=k, col=col: E.matmul(
                        psum[5][:, 0:32], q_fm[k][:, col:col + 128],
                        gw_s[:, 32 * k:32 * (k + 1)], start=(k == 0), stop=(k == 7)),
                      reads=(f"qfm{k}", "gws"), writes=("ps5",), dbg="gates mm")
                gt = gT[g % 4]
                gkey = f"gT{g % 4}"
                P("act", lambda E, gt=gt: E.activation(gt[:], psum[5][:, 0:32], AF.Exp, scale=cm1),
                  reads=("ps5", "cvec"), writes=(gkey,), dbg="gates exp")
                P("dve", lambda E, gt=gt: E.tensor_scalar(gt[:], gt[:], 1.0, None, OP.add),
                  reads=(gkey,), writes=(gkey,), dbg="gates 1+")
                P("dve", lambda E, gt=gt: E.reciprocal(gt[:], gt[:]),
                  reads=(gkey,), writes=(gkey,), dbg="gates recip")
                if DBG:
                    dbg_store(gkey, dbgg_h[g * 128:(g + 1) * 128, :], gt[:])

        def be_heads(g, plist, after_pair=None):
            mask = maskv if g < 2 else mask2
            mkey = "maskv" if g < 2 else "mask2"
            col = (g % 2) * 128

            def emit_score_pair(p):
                bb0 = (0, 1) if p % 2 == 0 else (6, 7)
                for s_ in range(3):
                    slot = (g - 2 + s_) % KR
                    for hi in range(2):
                        rowb = 64 * hi
                        sbank = bb0[hi]
                        P("pe", lambda E, p=p, rowb=rowb, sbank=sbank, s_=s_, slot=slot: E.matmul(
                            psum[sbank][:, 128 * s_:128 * (s_ + 1)],
                            kb[p][rowb:rowb + 64, 128 * slot:128 * slot + 128],
                            q_fm[p][rowb:rowb + 64, col:col + 128],
                            start=True, stop=True, tile_position=(rowb, 0)),
                          reads=(f"qfm{p}", f"kb{p}"), writes=(f"ps{sbank}",), dbg="score")

            def emit_softmax(h, sbank):
                P("dve", lambda E, sbank=sbank, mask=mask: E.tensor_tensor(
                    psum[sbank][:, 0:384], psum[sbank][:, 0:384], mask[:], OP.add),
                  reads=(f"ps{sbank}", mkey), writes=(f"ps{sbank}",), dbg="mask add")
                P("act", lambda E, sbank=sbank, h=h: E.activation(
                    PTs[h % 2][:], psum[sbank][:, 0:384], AF.Exp, scale=c0125, bias=cm24),
                  reads=(f"ps{sbank}", "cvec"), writes=(f"PTs{h%2}",), dbg="exp")

            def emit_pv(h):
                lbank = 4 + h // 8
                for s_ in range(3):
                    slot = (g - 2 + s_) % KR
                    P("pe", lambda E, h=h, s_=s_, slot=slot, lbank=lbank: E.matmul(
                        psum[lbank][:, (h % 8) * 64:(h % 8) * 64 + 64],
                        PTs[h % 2][:, 128 * s_:128 * (s_ + 1)],
                        vb[slot][:, h * 64:h * 64 + 64],
                        start=(s_ == 0), stop=(s_ == 2)),
                      reads=(f"PTs{h%2}", f"vb{slot}"), writes=(f"ps{lbank}",), dbg="PV")
                    P("pe", lambda E, h=h, s_=s_: E.matmul(
                        psum[2][:, h:h + 1], PTs[h % 2][:, 128 * s_:128 * (s_ + 1)],
                        c1, start=(s_ == 0), stop=(s_ == 2)),
                      reads=(f"PTs{h%2}", "cvec"), writes=("ps2",), dbg="Psum mm")

            for p in plist:
                a, b = 2 * p, 2 * p + 1
                emit_score_pair(p)
                if p >= 1:
                    emit_pv(a - 2)
                    emit_pv(b - 2)
                bb0 = (0, 1) if p % 2 == 0 else (6, 7)
                emit_softmax(a, bb0[0])
                emit_softmax(b, bb0[1])
                if after_pair is not None:
                    after_pair()
            if plist[-1] == NH // 2 - 1:
                emit_pv(NH - 2)
                emit_pv(NH - 1)
                P("act", lambda E: E.copy(sm[:, 40:56], psum[2][:, 0:16]),
                  reads=("ps2",), writes=("smsums",), dbg="smsums evac")

        def be_mix(g, ktffn_cb=None, yffn_cb=None):
            gkey = f"gT{g % 4}"
            gt = gT[g % 4]
            col = (g % 2) * 128
            P("dve", lambda E: E.reciprocal(sm[:, 56:72], sm[:, 40:56]),
              reads=("smsums",), writes=("smfac",), dbg="recip")
            P("dve", lambda E, gt=gt: E.tensor_mul(
                sm[:, 56:72], sm[:, 56:72], bass.AP(gt, 0, [[32, 128], [2, 16]])),
              reads=("smfac", gkey), writes=("smfac",), dbg="gl*recip")
            for half in range(2):
                lb = 4 + half
                glview = bass.AP(sm, 56 + 8 * half, [[SMW, 128], [1, 8], [0, 64]])
                P("dve", lambda E, lb=lb, half=half, glview=glview: E.tensor_tensor(
                    mix_tm[:, 512 * half:512 * (half + 1)], psum[lb][:, :], glview, OP.mult),
                  reads=(f"ps{lb}", "smfac"), writes=("mixtm",), dbg="mix gl")
            for h in range(LH):
                c = h // 2
                rowb = (h % 2) * 64
                bank = h % 2
                r = h // 2
                P("pe", lambda E, c=c, rowb=rowb, bank=bank, r=r: E.matmul(
                    psum[bank][:, 64 * r:64 * (r + 1)],
                    gq[c][rowb:rowb + 64, col:col + 128],
                    gk[c][rowb:rowb + 64, :],
                    start=(r == 0), stop=(r == 3), tile_position=(rowb, 0)),
                  reads=(f"gq{c}", f"gk{c}"), writes=(f"ps{bank}",), dbg="lsc")
            for par in range(2):
                bank = par
                P("dve", lambda E, bank=bank, g=g: E.tensor_tensor(
                    psum[bank][:, 0:256], psum[bank][:, 0:256],
                    bass.AP(lms, 64 * g, [[1024, 128], [0, 4], [1, 64]]), OP.add),
                  reads=(f"ps{bank}", "lms"), writes=(f"ps{bank}",), dbg="lmask add")
                P("act", lambda E, bank=bank, par=par: E.activation(
                    bass.AP(lp, 64 * par, [[512, 128], [128, 4], [1, 64]]),
                    psum[bank][:, 0:256], AF.Exp, scale=c0125, bias=cm12),
                  reads=(f"ps{bank}", "cvec"), writes=("lp",), dbg="lexp")
            if ktffn_cb is not None:
                ktffn_cb()
            P("dve", lambda E: E.reduce_sum(sm[:, 8:16], lp[:].rearrange("p (h n) -> p h n", n=64),
                                            axis=mybir.AxisListType.X),
              reads=("lp",), writes=("sm8",), dbg="lsum")
            P("dve", lambda E: E.reciprocal(sm[:, 8:16], sm[:, 8:16]),
              reads=("sm8",), writes=("sm8",), dbg="lrecip")
            P("dve", lambda E: E.tensor_tensor(
                lp[:], lp[:], bass.AP(sm, 8, [[SMW, 128], [1, 8], [0, 64]]), OP.mult),
              reads=("lp", "sm8"), writes=("lp",), dbg="lp scale")
            def lpT_ap(cc):
                if cc < 2:
                    return lpT[cc][:], f"lpT{cc}"
                return PTs[cc - 2][:, 256:384], f"PTs{cc - 2}"
            for cc in range(4):
                bank = 2 + cc % 2
                P("pe", lambda E, cc=cc, bank=bank: E.matmul(
                    psum[bank][:, 0:128], lp[:, 128 * cc:128 * (cc + 1)], ident[:],
                    is_transpose=True, start=True, stop=True),
                  reads=("lp", "ident"), writes=(f"ps{bank}",), dbg="lpT")
                ap_, key_ = lpT_ap(cc)
                P("act", lambda E, ap_=ap_, bank=bank: E.copy(ap_, psum[bank][:, 0:128]),
                  reads=(f"ps{bank}",), writes=(key_,), dbg="lpT evac")
            for cc in range(4):
                ap_, key_ = lpT_ap(cc)
                P("pe", lambda E, ap_=ap_, cc=cc: E.matmul(
                    psum[0][0:64, 128 * cc:128 * cc + 128],
                    gvd[0:64, 128 * cc:128 * cc + 64], ap_[0:64, :],
                    start=True, stop=True),
                  reads=("gvd", key_), writes=("ps0",), dbg="rem A")
                P("pe", lambda E, ap_=ap_, cc=cc: E.matmul(
                    psum[0][64:128, 128 * cc:128 * cc + 128],
                    gvd[64:128, 128 * cc + 64:128 * cc + 128], ap_[64:128, :],
                    start=True, stop=True, tile_position=(64, 64)),
                  reads=("gvd", key_), writes=("ps0",), dbg="rem B")
            for ph in range(2):
                P("act", lambda E, ph=ph: E.copy(rem_fm[:, 256 * ph:256 * (ph + 1)],
                                                 psum[0][:, 256 * ph:256 * (ph + 1)]),
                  reads=("ps0",), writes=("remfm", f"remfm{ph}"), dbg="rem evac")
            for k in range(4):
                for half in range(2):
                    b = 2 + half
                    P("pe", lambda E, k=k, half=half, b=b: E.matmul(
                        psum[b][:, :], rem_fm[:, 128 * k:128 * (k + 1)],
                        rout_s[k][:, 512 * half:512 * (half + 1)],
                        start=(k == 0), stop=(k == 3)),
                      reads=(f"remfm{k//2}", f"routs{k}"), writes=(f"ps{b}",), dbg="remrout")
            if yffn_cb is not None:
                yffn_cb()
            for half in range(2):
                b = 2 + half
                grview = bass.AP(gt, 1 + 16 * half, [[32, 128], [2, 8], [0, 64]])
                P("dve", lambda E, b=b, half=half, grview=grview: E.tensor_tensor(
                    qD[:, 512 * half:512 * (half + 1)], psum[b][:, :], grview, OP.mult),
                  reads=(f"ps{b}", gkey), writes=QD, dbg="mix gr")
                P("dve", lambda E, half=half: E.tensor_tensor(
                    mix_tm[:, 512 * half:512 * (half + 1)],
                    mix_tm[:, 512 * half:512 * (half + 1)],
                    qD[:, 512 * half:512 * (half + 1)], OP.add),
                  reads=("mixtm",) + QD, writes=("mixtm",), dbg="mix add")
            if DBG:
                dbg_store("mixtm", dbgtm_h[3 * S + g * 128:3 * S + (g + 1) * 128, :], mix_tm[:])
        def be_wout(g):
            for half in range(2):
                bank = 2 + half
                for r in range(4):
                    c = half * 4 + r
                    P("pe", lambda E, c=c, bank=bank, r=r: E.matmul(
                        psum[bank][:, r * 128:(r + 1) * 128], mix_tm[:, 128 * c:128 * (c + 1)],
                        ident[:], is_transpose=True, start=(r == 0), stop=(r == 3)),
                      reads=("mixtm", "ident"), writes=(f"ps{bank}",), dbg="mixT")

                def mevac(E, half=half, bank=bank):
                    last = None
                    for r in range(4):
                        c = half * 4 + r
                        last = E.copy(mix_fm[:, 128 * c:128 * (c + 1)],
                                      psum[bank][:, r * 128:(r + 1) * 128])
                    return last
                P("act", mevac, reads=(f"ps{bank}",), writes=("mixfm", f"mixfm{half}"),
                  dbg="mixT evac")
            for k in range(8):
                for half in range(2):
                    b = 4 + half
                    P("pe", lambda E, k=k, half=half, b=b: E.matmul(
                        psum[b][:, :], mix_fm[:, 128 * k:128 * (k + 1)],
                        wout_s[k][:, 512 * half:512 * (half + 1)],
                        start=(k == 0), stop=(k == 7)),
                      reads=(f"mixfm{k//4}", f"wouts{k}"), writes=(f"ps{b}",), dbg="wout")

        n_stores = [0]

        def epi_evac(g):
            for half in range(2):
                b = 4 + half
                P("act", lambda E, b=b, half=half: E.copy(
                    attnout[:, 512 * half:512 * (half + 1)], psum[b][:, :]),
                  reads=(f"ps{b}",), writes=("mixfm", f"mixfm{half}"), dbg="attnout evac")
                P("act", lambda E, b=b, half=half: E.activation(
                    qA[:, 512 * half:512 * (half + 1)], psum[b][:, :], AF.Square),
                  reads=(f"ps{b}",), writes=(f"qA{half}",), dbg="ln2 sq")
            if DBG:
                dbg_store("mixfm", dbgtm_h[S + g * 128:S + (g + 1) * 128, :], attnout[:])

        def epi_segs(g):
            segs = []

            def Sg(eng, fn, **kw):
                segs.append(lambda eng=eng, fn=fn, kw=kw: P(eng, fn, **kw))

            xr = x_ring[g % XR]
            xkey = f"xr{g % XR}"
            Sg("dve", lambda E: E.reduce_sum(smc(24), attnout[:], axis=mybir.AxisListType.X),
               reads=("mixfm",), writes=("sm24",), dbg="ln2 s1")
            Sg("dve", lambda E: E.reduce_sum(smc(25), qA[:], axis=mybir.AxisListType.X),
               reads=QA, writes=("sm25",), dbg="ln2 s2")
            Sg("dve", lambda E: E.tensor_scalar(smc(24), smc(24), 1.0 / D, None, OP.mult),
               reads=("sm24",), writes=("sm24",), dbg="ln2 m")
            Sg("dve", lambda E: E.scalar_tensor_tensor(smc(26), smc(24), 1.0, smc(24), OP.mult, OP.mult),
               reads=("sm24",), writes=("sm26",), dbg="ln2 m2")
            Sg("dve", lambda E: E.scalar_tensor_tensor(smc(25), smc(25), 1.0 / D, smc(26), OP.mult, OP.subtract),
               reads=("sm25", "sm26"), writes=("sm25",), dbg="ln2 var")
            Sg("dve", lambda E: E.tensor_scalar(smc(25), smc(25), 1.0e-5, None, OP.add),
               reads=("sm25",), writes=("sm25",), dbg="ln2 +eps")
            Sg("act", lambda E: E.activation(smc(26), smc(25), AF.Ln),
               reads=("sm25",), writes=("sm26",), dbg="ln2 ln")
            Sg("act", lambda E: E.activation(smc(26), smc(26), AF.Exp, scale=cmhalf),
               reads=("sm26", "cvec"), writes=("sm26",), dbg="ln2 rsqrt")
            Sg("dve", lambda E: E.scalar_tensor_tensor(smc(27), smc(24), -1.0, smc(26), OP.mult, OP.mult),
               reads=("sm24", "sm26"), writes=("sm27",), dbg="ln2 bb")
            Sg("dve", lambda E: E.tensor_scalar(qD[:], attnout[:], smc(26), smc(27), OP.mult, OP.add),
               reads=("mixfm", "sm26", "sm27"), writes=QD, dbg="ln2 apply")
            Sg("dve", lambda E, xr=xr: E.tensor_add(y1[:], qD[:], xr[:]),
               reads=QD + (xkey,), writes=("y1",), dbg="y1")
            y1_pos = len(segs)
            if DBG:
                segs.append(lambda g=g: dbg_store(
                    "y1", dbgtm_h[2 * S + g * 128:2 * S + (g + 1) * 128, :], y1[:]))
            Sg("dve", lambda E: E.reduce_sum(smc(28), y1[:], axis=mybir.AxisListType.X),
               reads=("y1",), writes=("sm28",), dbg="ln3 s1")
            Sg("dve", lambda E: E.tensor_scalar(smc(28), smc(28), 1.0 / D, None, OP.mult),
               reads=("sm28",), writes=("sm28",), dbg="ln3 m")
            Sg("dve", lambda E: E.tensor_scalar(qD[:], y1[:], smc(28), None, OP.subtract),
               reads=("y1", "sm28"), writes=QD, dbg="u")
            Sg("dve", lambda E: E.reduce_max(smc(31), qD[:], axis=mybir.AxisListType.X,
                                             apply_absolute_value=True),
               reads=QD, writes=("sm31",), dbg="amax_u")
            Sg("dve", lambda E: E.reciprocal(smc(32), smc(31)),
               reads=("sm31",), writes=("sm32",), dbg="amax recip")
            for half in range(2):
                hs = slice(512 * half, 512 * (half + 1))
                Sg("dve", lambda E, hs=hs: E.tensor_scalar(qA[:, hs], qD[:, hs], smc(32), 127.0,
                                                           OP.mult, OP.mult),
                   reads=(QD[half], "sm32"), writes=(QA[half],), dbg="m=u*s")
                Sg("gps", lambda E, hs=hs: E.tensor_copy(qI[:, hs], qA[:, hs]),
                   reads=(QA[half],), writes=(QI[half],), dbg="k0 rint")
            for half in range(2):
                hs = slice(512 * half, 512 * (half + 1))
                Sg("dve", lambda E, hs=hs: E.tensor_copy(qD[:, hs], qI[:, hs]),
                   reads=(QI[half],), writes=(QD[half],), dbg="k0f")
                Sg("dve", lambda E, hs=hs: E.tensor_sub(qA[:, hs], qA[:, hs], qD[:, hs]),
                   reads=(QA[half], QD[half]), writes=(QA[half],), dbg="d")
                Sg("dve", lambda E, hs=hs: E.scalar_tensor_tensor(qD[:, hs], qA[:, hs], 0.5,
                                                                  qD[:, hs], OP.is_gt, OP.add),
                   reads=(QA[half], QD[half]), writes=(QD[half],), dbg="k1")
                Sg("dve", lambda E, hs=hs: E.tensor_scalar(qA[:, hs], qA[:, hs], -0.5, None, OP.is_lt),
                   reads=(QA[half],), writes=(QA[half],), dbg="e2")
                Sg("dve", lambda E, hs=hs: E.tensor_sub(qD[:, hs], qD[:, hs], qA[:, hs]),
                   reads=(QD[half], QA[half]), writes=(QD[half],), dbg="k final")
            Sg("act", lambda E: E.activation(mix_tm[:], y1[:], AF.Square),
               reads=("y1",), writes=("mixtm",), dbg="ln3 sq")
            Sg("dve", lambda E: E.reduce_sum(smc(29), mix_tm[:], axis=mybir.AxisListType.X),
               reads=("mixtm",), writes=("sm29",), dbg="ln3 s2")
            Sg("dve", lambda E: E.scalar_tensor_tensor(smc(30), smc(28), 1.0, smc(28), OP.mult, OP.mult),
               reads=("sm28",), writes=("sm30",), dbg="ln3 m2")
            Sg("dve", lambda E: E.scalar_tensor_tensor(smc(29), smc(29), 1.0 / D, smc(30), OP.mult, OP.subtract),
               reads=("sm29", "sm30"), writes=("sm29",), dbg="ln3 var")
            Sg("dve", lambda E: E.tensor_scalar(smc(29), smc(29), 1.0e-5, None, OP.add),
               reads=("sm29",), writes=("sm29",), dbg="ln3 +eps")
            Sg("act", lambda E: E.activation(smc(30), smc(29), AF.Ln),
               reads=("sm29",), writes=("sm30",), dbg="ln3 ln")
            Sg("act", lambda E: E.activation(smc(30), smc(30), AF.Exp, scale=cmhalf),
               reads=("sm30", "cvec"), writes=("sm30",), dbg="ln3 rsqrt")
            Sg("dve", lambda E: E.tensor_mul(smc(33), smc(31), smc(30)),
               reads=("sm31", "sm30"), writes=("sm33",), dbg="amax_h2")
            Sg("dve", lambda E: E.tensor_scalar(smc(33), smc(33), 1.0e-5, None, OP.max),
               reads=("sm33",), writes=("sm33",), dbg="amax clamp")
            Sg("dve", lambda E: E.tensor_scalar(smc(33), smc(33), ws_col[:], 1.0 / 127.0,
                                                OP.mult, OP.mult),
               reads=("sm33", "wscol"), writes=("sm33",), dbg="fscale")
            return segs, y1_pos

        def epi_ktffn(g):
            for half in range(2):
                bank = 2 + half
                for r in range(4):
                    c = half * 4 + r
                    P("pe", lambda E, c=c, bank=bank, r=r: E.matmul(
                        psum[bank][:, r * 128:(r + 1) * 128], qD[:, 128 * c:128 * (c + 1)],
                        ident[:], is_transpose=True, start=(r == 0), stop=(r == 3)),
                      reads=(QD[half], "ident"), writes=(f"ps{bank}",), dbg="kT")

                def kevac(E, half=half, bank=bank):
                    last = None
                    for r in range(4):
                        c = half * 4 + r
                        last = E.copy(kfm[c][:], psum[bank][:, r * 128:(r + 1) * 128])
                    return last
                P("act", kevac, reads=(f"ps{bank}",),
                  writes=tuple(f"kfm{half*4+r}" for r in range(4)), dbg="kT evac")
            for k in range(8):
                for half in range(2):
                    b = 6 + half
                    P("pe", lambda E, k=k, half=half, b=b: E.matmul(
                        psum[b][:, :], kfm[k][:], tern_s[k][:, 512 * half:512 * (half + 1)],
                        start=(k == 0), stop=(k == 7)),
                      reads=(f"kfm{k}", f"terns{k}"), writes=(f"ps{b}",), dbg="ffn mm")

        def epi_yffn(g):
            for half in range(2):
                b = 6 + half
                P("dve", lambda E, b=b, half=half: E.scalar_tensor_tensor(
                    qA[:, 512 * half:512 * (half + 1)], psum[b][:, :], smc(33),
                    bias_bc[:, 512 * half:512 * (half + 1)], OP.mult, OP.add),
                  reads=(f"ps{b}", "sm33", "biasbc"), writes=(QA[half],), dbg="yffn")

        def epi_pe_mm(g):
            epi_ktffn(g)
            epi_yffn(g)

        def epi_ln4(g):
            P("dve", lambda E: E.reduce_sum(smc(34), qA[:], axis=mybir.AxisListType.X),
              reads=QA, writes=("sm34",), dbg="ln4 s1")
            P("act", lambda E: E.activation(qD[:], qA[:], AF.Square),
              reads=QA, writes=QD, dbg="ln4 sq")
            P("dve", lambda E: E.reduce_sum(smc(35), qD[:], axis=mybir.AxisListType.X),
              reads=QD, writes=("sm35",), dbg="ln4 s2")
            P("dve", lambda E: E.tensor_scalar(smc(34), smc(34), 1.0 / D, None, OP.mult),
              reads=("sm34",), writes=("sm34",), dbg="ln4 m")
            P("dve", lambda E: E.scalar_tensor_tensor(smc(36), smc(34), 1.0, smc(34), OP.mult, OP.mult),
              reads=("sm34",), writes=("sm36",), dbg="ln4 m2")
            P("dve", lambda E: E.scalar_tensor_tensor(smc(35), smc(35), 1.0 / D, smc(36), OP.mult, OP.subtract),
              reads=("sm35", "sm36"), writes=("sm35",), dbg="ln4 var")
            P("dve", lambda E: E.tensor_scalar(smc(35), smc(35), 1.0e-5, None, OP.add),
              reads=("sm35",), writes=("sm35",), dbg="ln4 +eps")
            P("act", lambda E: E.activation(smc(36), smc(35), AF.Ln),
              reads=("sm35",), writes=("sm36",), dbg="ln4 ln")
            P("act", lambda E: E.activation(smc(36), smc(36), AF.Exp, scale=cmhalf),
              reads=("sm36", "cvec"), writes=("sm36",), dbg="ln4 rsqrt")
            P("dve", lambda E: E.scalar_tensor_tensor(smc(37), smc(34), -1.0, smc(36), OP.mult, OP.mult),
              reads=("sm34", "sm36"), writes=("sm37",), dbg="ln4 bb")
            P("dve", lambda E: E.tensor_scalar(qD[:], qA[:], smc(36), smc(37), OP.mult, OP.add),
              reads=QA + ("sm36", "sm37"), writes=QD, dbg="ln4 apply")

        def epi_y2(g):
            P("dve", lambda E: E.tensor_add(attnout[:], qD[:], y1[:]),
              reads=QD + ("y1",), writes=("mixfm", "mixfm0", "mixfm1"), dbg="y2")
            n_stores[0] += 1
            dma_store("mixfm", y_hbm[g * 128:(g + 1) * 128, :], attnout[:])

        pend = {"segs": [], "y1pos": 0, "done": 0}

        def drain(n):
            s = pend["segs"]
            stop = min(pend["done"] + n, len(s))
            for i in range(pend["done"], stop):
                s[i]()
            pend["done"] = stop

        if SCHED == "seq":
            for b0 in range(0, NQT, 2):
                fe_A(b0)
                fe_A(b0 + 1)
                fe_Qpair(b0)
                fe_KVpair(b0)
                fe_GQgpair(b0)
                for g in (b0, b0 + 1):
                    be_heads(g, list(range(0, 4)))
                    be_heads(g, list(range(4, 8)))
                    if g == 0:
                        dma_load("maskv", maskv[:], m1_h[:, :])
                    be_mix(g)
                    be_wout(g)
                    epi_evac(g)
                    pend["segs"], pend["y1pos"] = epi_segs(g)
                    pend["done"] = 0
                    drain(10**9)
                    epi_pe_mm(g)
                    epi_ln4(g)
                    epi_y2(g)
        else:
            fe_A(0)
            fe_A(1)
            fe_Qpair(0)
            fe_KVpair(0)
            fe_GQgpair(0)
            dr5 = lambda: drain(5)
            for g in range(NQT):
                be_heads(g, list(range(0, 2)), after_pair=dr5)
                if g + 2 < NQT:
                    fe_A_ln(g + 2)
                be_heads(g, list(range(2, 4)), after_pair=dr5)
                be_heads(g, list(range(4, 8)), after_pair=dr5)
                if g == 0:
                    dma_load("maskv", maskv[:], m1_h[:, :])
                drain(10**9)
                if g + 2 < NQT:
                    fe_A_hT(g + 2)

                def fill(g=g):
                    if g % 2 == 1 and g + 2 < NQT:
                        fe_Qpair(g + 1)
                    if g >= 1:
                        epi_ktffn(g - 1)
                ycb = (lambda gg=g - 1: epi_yffn(gg)) if g >= 1 else None
                be_mix(g, ktffn_cb=fill, yffn_cb=ycb)
                if g >= 1:
                    epi_ln4(g - 1)
                be_wout(g)
                if g >= 1:
                    epi_y2(g - 1)
                epi_evac(g)
                if g % 2 == 1 and g + 2 < NQT:
                    fe_KVpair(g + 1)
                    fe_GQgpair(g + 1)
                pend["segs"], pend["y1pos"] = epi_segs(g)
                pend["done"] = 0
            drain(10**9)
            epi_pe_mm(NQT - 1)
            epi_ln4(NQT - 1)
            epi_y2(NQT - 1)

        eng_counts = {e: 0 for e in ("pe", "dve", "act", "gps")}
        stream_counts = {}
        stream_eng = {}
        needed = set()
        for n in plan.nodes:
            for d in n.waits.values():
                needed.add(id(d))
        for n in plan.nodes:
            if n.is_dma:
                c = stream_counts.get(n.stream, 0) + 16
                stream_counts[n.stream] = c
                stream_eng[n.stream] = n.eng
                n.inc_val = (n.stream, c)
            elif id(n) in needed:
                eng_counts[n.eng] += 1
                n.inc_val = (n.eng, eng_counts[n.eng])

        with nc.Block() as block:
            def emit_engine(eng, E):
                waited = {}
                for n in plan.by_eng[eng]:
                    for d in n.waits.values():
                        iv = d.inc_val
                        assert iv is not None, f"dep of {n.dbg} on {d.dbg} has no inc"
                        semname, val = iv
                        if waited.get(semname, 0) >= val:
                            continue
                        waited[semname] = val
                        E.wait_ge(sems[semname], val)
                    instr = n.fn(E)
                    if n.inc_val is not None:
                        semname, val = n.inc_val
                        instr.then_inc(sems[semname], 16 if n.is_dma else 1)
                for s_, cnt in stream_counts.items():
                    if stream_eng[s_] == eng:
                        E.wait_ge(sems[s_], cnt)

            @block.tensor
            def _(E):
                emit_engine("pe", E)

            @block.vector
            def _(E):
                emit_engine("dve", E)

            @block.scalar
            def _(E):
                emit_engine("act", E)

            @block.gpsimd
            def _(E):
                emit_engine("gps", E)

            @block.sync
            def _(E):
                emit_engine("sp", E)

    return nc


def _lmask_table():
    import ml_dtypes
    out = np.empty((2048, 64), dtype=ml_dtypes.bfloat16)
    for g in range(16):
        r = np.arange(128)[:, None]
        n = np.arange(64)[None, :]
        out[128 * g:128 * (g + 1)] = np.where(n <= 4 * g + r // 32, 0.0, NEG)
    return out


def _host_constants():
    import ml_dtypes
    ident = np.eye(128, dtype=np.float32)
    r = np.arange(128)[:, None]
    j = np.arange(384)[None, :]
    base = (j - r >= 1) & (j - r <= 256)
    m2 = np.where(base, 0.0, NEG).astype(ml_dtypes.bfloat16)
    m1 = np.where(base & (j >= 128), 0.0, NEG).astype(ml_dtypes.bfloat16)
    m0 = np.where(base & (j >= 256), 0.0, NEG).astype(ml_dtypes.bfloat16)

    def tr(m):
        out = np.empty_like(m)
        for s in range(3):
            out[:, 128 * s:128 * (s + 1)] = m[:, 128 * s:128 * (s + 1)].T
        return out
    return ident, tr(m0), tr(m1), tr(m2)


def kernel(L0_attn_x, L0_attn_wq, L0_attn_wk, L0_attn_wv, L0_attn_wgq, L0_attn_wgk,
           L0_attn_wgv, L0_attn_r_out_w, L0_attn_gate_local_w, L0_attn_gate_remote_w,
           L0_attn_w_out, L0_ffn_packed_weight, L0_ffn_weight_scale, L0_ffn_bias):
    x = np.ascontiguousarray(np.asarray(L0_attn_x, dtype=np.float32))
    ident, m0, m1, m2 = _host_constants()
    glw = np.asarray(L0_attn_gate_local_w, dtype=np.float32)
    grw = np.asarray(L0_attn_gate_remote_w, dtype=np.float32)
    gwbd = np.zeros((D, 32), dtype=np.float32)
    for h in range(NH):
        gwbd[64 * h:64 * (h + 1), 2 * h] = glw[h]
        gwbd[64 * h:64 * (h + 1), 2 * h + 1] = grw[h]
    common = dict(
        wq=np.ascontiguousarray(L0_attn_wq, dtype=np.float32),
        wk=np.ascontiguousarray(L0_attn_wk, dtype=np.float32),
        wv=np.ascontiguousarray(L0_attn_wv, dtype=np.float32),
        wgq=np.ascontiguousarray(L0_attn_wgq, dtype=np.float32),
        wgk=np.ascontiguousarray(L0_attn_wgk, dtype=np.float32),
        wgv=np.ascontiguousarray(L0_attn_wgv, dtype=np.float32),
        rout=np.ascontiguousarray(L0_attn_r_out_w, dtype=np.float32),
        wout=np.ascontiguousarray(L0_attn_w_out, dtype=np.float32),
        gwbd=gwbd,
        packed=np.ascontiguousarray(L0_ffn_packed_weight, dtype=np.int32),
        wscale=np.ascontiguousarray(L0_ffn_weight_scale, dtype=np.float32),
        fbias=np.ascontiguousarray(L0_ffn_bias, dtype=np.float32),
        ident=ident, m0=m0, m1=m1, m2=m2,
        zeros=np.zeros((128, 1024), dtype=np.float32),
        cvec=np.tile(np.array([0.125, -24.0, -12.0, -0.5, 0.0, -1.0, 1.0, 0.0],
                              dtype=np.float32), (128, 1)),
        lmasks=_lmask_table(),
    )
    in_maps = [dict(common, x=np.ascontiguousarray(x[b])) for b in range(8)]
    nc = build_program()
    res = run_bass_kernel_spmd(nc, in_maps, core_ids=list(range(8)))
    out = np.stack([np.asarray(r["y"], dtype=np.float32) for r in res.results], axis=0)
    return out


# version 31 (final)
# speedup vs baseline: 1.1194x; 1.1164x over previous
import contextlib
import numpy as np

import concourse.bass as bass
import concourse.mybir as mybir
from concourse.bass_utils import run_bass_kernel_spmd
from concourse.alu_op_type import AluOpType as OP

F32 = mybir.dt.float32
BF16 = mybir.dt.bfloat16
I32 = mybir.dt.int32
AF = mybir.ActivationFunctionType

S = 2048
D = 1024
NH = 16
HD = 64
LH = 8
LD = 512
NLAT = 64
CH = 32
NQT = 16
NEG = -30000.0
KR = 4
XR = 4
RW = 4
RG = 3

DBG = False
SCHED = "il"
ENGINES = ("pe", "dve", "act", "gps", "sp")
QA = ("qA0", "qA1")
QD = ("qD0", "qD1")
QI = ("qI0", "qI1")


class Node:
    __slots__ = ("eng", "fn", "waits", "inc_val", "is_dma", "seq", "dbg", "eng_pos",
                 "stream")

    def __init__(self, eng, fn, seq, dbg):
        self.eng = eng
        self.fn = fn
        self.waits = {}
        self.inc_val = None
        self.is_dma = False
        self.seq = seq
        self.dbg = dbg
        self.eng_pos = 0
        self.stream = None


class Plan:
    GAP = 10**9

    def __init__(self):
        self.nodes = []
        self.by_eng = {e: [] for e in ENGINES}
        self.last_w = {}
        self.readers = {}
        self.stream_last = {}

    def add(self, eng, fn, reads=(), writes=(), is_dma=False, dbg=""):
        n = Node(eng, fn, len(self.nodes), dbg)
        n.is_dma = is_dma
        n.eng_pos = len(self.by_eng[eng])
        deps = []
        for b in reads:
            w = self.last_w.get(b)
            if w is not None:
                deps.append(w)
        for b in writes:
            w = self.last_w.get(b)
            if w is not None:
                deps.append(w)
            deps.extend(self.readers.get(b, ()))
        for d in deps:
            if d is n:
                continue
            if eng == "pe" and d.eng == "pe" and not d.is_dma:
                continue
            if d.eng == eng and not d.is_dma and n.eng_pos - d.eng_pos > self.GAP:
                continue
            if id(d) not in n.waits:
                n.waits[id(d)] = d
        for b in reads:
            self.readers.setdefault(b, []).append(n)
        for b in writes:
            self.last_w[b] = n
            self.readers[b] = []
        self.nodes.append(n)
        self.by_eng[eng].append(n)
        return n


def build_program():
    nc = bass.Bass("TRN2", target_bir_lowering=False, detect_race_conditions=False)

    x_hbm = nc.dram_tensor("x", [S, D], F32, kind="ExternalInput").ap()
    wq_h = nc.dram_tensor("wq", [D, D], F32, kind="ExternalInput").ap()
    wk_h = nc.dram_tensor("wk", [D, D], F32, kind="ExternalInput").ap()
    wv_h = nc.dram_tensor("wv", [D, D], F32, kind="ExternalInput").ap()
    wgq_h = nc.dram_tensor("wgq", [D, LD], F32, kind="ExternalInput").ap()
    wgk_h = nc.dram_tensor("wgk", [D, LD], F32, kind="ExternalInput").ap()
    wgv_h = nc.dram_tensor("wgv", [D, LD], F32, kind="ExternalInput").ap()
    rout_h = nc.dram_tensor("rout", [LD, D], F32, kind="ExternalInput").ap()
    wout_h = nc.dram_tensor("wout", [D, D], F32, kind="ExternalInput").ap()
    gwbd_h = nc.dram_tensor("gwbd", [D, 32], F32, kind="ExternalInput").ap()
    packed_h = nc.dram_tensor("packed", [262144], I32, kind="ExternalInput").ap()
    ws_h = nc.dram_tensor("wscale", [1], F32, kind="ExternalInput").ap()
    bias_h = nc.dram_tensor("fbias", [D], F32, kind="ExternalInput").ap()
    ident_h = nc.dram_tensor("ident", [128, 128], F32, kind="ExternalInput").ap()
    m0_h = nc.dram_tensor("m0", [128, 384], BF16, kind="ExternalInput").ap()
    m1_h = nc.dram_tensor("m1", [128, 384], BF16, kind="ExternalInput").ap()
    m2_h = nc.dram_tensor("m2", [128, 384], BF16, kind="ExternalInput").ap()
    zeros_h = nc.dram_tensor("zeros", [128, 1024], F32, kind="ExternalInput").ap()
    cvec_h = nc.dram_tensor("cvec", [128, 8], F32, kind="ExternalInput").ap()
    lmasks_h = nc.dram_tensor("lmasks", [2048, 64], BF16, kind="ExternalInput").ap()
    y_hbm = nc.dram_tensor("y", [S, D], F32, kind="ExternalOutput").ap()
    if DBG:
        dbgfm_h = nc.dram_tensor("dbgfm", [3 * 16384, 128], F32, kind="ExternalOutput").ap()
        dbgtm_h = nc.dram_tensor("dbgtm", [4 * S, D], F32, kind="ExternalOutput").ap()
        dbgg_h = nc.dram_tensor("dbgg", [S, 32], F32, kind="ExternalOutput").ap()

    es = contextlib.ExitStack()
    sb = lambda name, shape, dt=F32: es.enter_context(nc.sbuf_tensor(name, shape, dt))
    ps = lambda name: es.enter_context(nc.psum_tensor(name, [128, 512], F32))

    with es:
        h_fm = [sb(f"hfm{c}", [128, 256]) for c in range(8)]
        q_fm = [sb(f"qfm{c}", [128, 256]) for c in range(8)]
        gq = [sb(f"gq{c}", [128, 256]) for c in range(4)]
        kb = [sb(f"kb{c}", [128, 128 * KR]) for c in range(8)]
        vb = [sb(f"vb{s}", [128, D]) for s in range(KR)]
        wout_s = [sb(f"wouts{k}", [128, D]) for k in range(8)]
        rout_s = [sb(f"routs{k}", [128, D]) for k in range(4)]
        tern_s = [sb(f"terns{k}", [128, D], BF16) for k in range(8)]
        ring_w = [sb(f"ringw{i}", [128, D]) for i in range(RW)]
        ring_g = [sb(f"ringg{i}", [128, LD]) for i in range(RG)]
        pooled = [sb(f"pool{c}", [128, NLAT]) for c in range(8)]
        gk = [sb(f"gk{c}", [128, NLAT]) for c in range(4)]
        gvd = sb("gvd", [128, LD])
        gw_s = sb("gws", [128, 256])
        maskv = sb("maskv", [128, 384], BF16)
        mask2 = sb("mask2t", [128, 384], BF16)
        lms = sb("lms", [128, 1024], BF16)
        ident = sb("identt", [128, 128])
        bias_bc = sb("biasbc", [128, D])
        ws_col = sb("wscol", [128, 1])
        gT = [sb(f"gT{i}", [128, 32]) for i in range(4)]
        x_ring = [sb(f"xr{i}", [128, D]) for i in range(XR)]
        h_tm = sb("htm", [128, D])
        PTs = [sb(f"PTs{i}", [128, 384]) for i in range(2)]
        lp = sb("lp", [128, 512])
        lpT3 = [sb(f"lpT{c}", [128, 128]) for c in range(2)]
        lpT = lpT3 + [None, None]
        rem_fm = sb("remfm", [128, 512])
        mix_tm = sb("mixtm", [128, D])
        mix_fm = sb("mixfm", [128, D])
        attnout = mix_fm
        y1 = sb("y1", [128, D])
        qA = sb("qA", [128, D])
        qD = sb("qD", [128, D])
        qI = sb("qI", [128, D], I32)
        kfm = [sb(f"kfm{k}", [128, 128], BF16) for k in range(8)]
        tp = qI[:]
        tfw = h_tm[:]
        tern_om = qA
        SMW = 72
        sm = sb("small", [128, SMW])
        smc = lambda i: sm[:, i:i + 1]
        cvec = sb("cvec_s", [128, 8])
        c0125 = cvec[:, 0:1]
        cm24 = cvec[:, 1:2]
        cm12 = cvec[:, 2:3]
        cmhalf = cvec[:, 3:4]
        cm1 = cvec[:, 5:6]
        c1 = cvec[:, 6:7]

        psum = [ps(f"pp{i}") for i in range(8)]

        STREAMS = tuple(f"rw{i}" for i in range(RW)) + \
            tuple(f"rg{i}" for i in range(RG)) + ("mi0", "mi1", "st", "db0", "db1")
        sems = {}
        for e in ("pe", "dve", "act", "gps"):
            sems[e] = es.enter_context(nc.semaphore(f"sem_{e}"))
        for s_ in STREAMS:
            sems[s_] = es.enter_context(nc.semaphore(f"sem_{s_}"))

        plan = Plan()
        P = plan.add
        mi_rr = [0]

        def dma_node(eng, stream, fn, reads, writes, dbg):
            n = P(eng, fn, reads=reads, writes=writes, is_dma=True, dbg=dbg)
            n.stream = stream
            prev = plan.stream_last.get(stream)
            if prev is not None:
                n.waits[id(prev)] = prev
            plan.stream_last[stream] = n
            return n

        def dma_load(buf_keys, out_ap, in_ap, extra_reads=(), stream=None):
            if isinstance(buf_keys, str):
                buf_keys = (buf_keys,)
            if stream is None:
                stream = f"mi{mi_rr[0] % 2}"
                mi_rr[0] += 1

            def f(E, out_ap=out_ap, in_ap=in_ap):
                return E.dma_start(out=out_ap, in_=in_ap)
            return dma_node("sp", stream, f, extra_reads, buf_keys,
                            f"load {buf_keys[0]}")

        db_rr = [0]

        def dma_store(in_key, out_ap, in_ap, stream="st"):
            def f(E, out_ap=out_ap, in_ap=in_ap):
                return E.dma_start(out=out_ap, in_=in_ap)
            return dma_node("act", stream, f, (in_key,), (), f"store {in_key}")

        def dbg_store(in_key, out_ap, in_ap):
            s_ = f"db{db_rr[0] % 2}"
            db_rr[0] += 1
            return dma_store(in_key, out_ap, in_ap, stream=s_)

        dma_load("ident", ident[:], ident_h[:, :])
        dma_load("maskv", maskv[:], m0_h[:, :])
        dma_load("mask2", mask2[:], m2_h[:, :])
        for k in range(8):
            dma_load("gws", gw_s[:, 32 * k:32 * (k + 1)], gwbd_h[128 * k:128 * (k + 1), :])
        dma_load("biasbc", bias_bc[:], bass.AP(bias_h.tensor, 0, [[0, 128], [1, D]]))
        dma_load("wscol", ws_col[:], bass.AP(ws_h.tensor, 0, [[0, 128], [1, 1]]))
        dma_load("cvec", cvec[:], cvec_h[:, :])
        for g in range(16):
            dma_load("lms", lms[:, 64 * g:64 * (g + 1)], lmasks_h[128 * g:128 * (g + 1), :])
        for k in range(8):
            dma_load(f"wouts{k}", wout_s[k][:], wout_h[128 * k:128 * (k + 1), :])
        for k in range(4):
            dma_load(f"routs{k}", rout_s[k][:], rout_h[128 * k:128 * (k + 1), :])
        for c in range(8):
            dma_load(f"pool{c}", pooled[c][:], zeros_h[:, 0:NLAT])
        for c in range(4):
            dma_load(f"gk{c}", gk[c][:], zeros_h[:, 0:NLAT])
        dma_load("gvd", gvd[:], zeros_h[:, 0:LD])
        for c in range(8):
            for s_ in (2, 3):
                dma_load(f"kb{c}", kb[c][:, 128 * s_:128 * (s_ + 1)], zeros_h[:, 0:128])
        for s_ in (2, 3):
            dma_load(f"vb{s_}", vb[s_][:], zeros_h[:, :])

        for d in range(8):
            dma_load(("tp",) + QI, tp,
                     bass.AP(packed_h.tensor, 131072 * (d % 2), [[1024, 128], [1, 1024]]))
            sh = 2 * (d // 2)
            if sh:
                P("dve", lambda E, sh=sh: E.tensor_scalar(tp, tp, sh, None, OP.logical_shift_right),
                  reads=("tp",) + QI, writes=("tp",) + QI, dbg="unpack shift")
            P("dve", lambda E: E.tensor_scalar(tp, tp, 3, None, OP.bitwise_and),
              reads=("tp",) + QI, writes=("tp",) + QI, dbg="unpack and")
            P("dve", lambda E: E.tensor_copy(tfw, tp),
              reads=("tp",) + QI, writes=("tfw", "htm"), dbg="unpack i2f")
            P("dve", lambda E: E.tensor_scalar(tern_om[:], tfw, 1.0, None, OP.subtract),
              reads=("tfw", "htm"), writes=("tom",) + QA, dbg="unpack sub")
            for c in range(8):
                bank = 2 + c % 2
                P("pe", lambda E, c=c, d=d, bank=bank: E.matmul(
                    psum[bank][:, 0:128], tern_om[:, 128 * c:128 * (c + 1)],
                    ident[:], is_transpose=True, start=True, stop=True),
                  reads=("tom", "ident") + QA, writes=(f"ps{bank}",), dbg="ternT transpose")
                P("act", lambda E, c=c, d=d, bank=bank: E.copy(
                    tern_s[c][:, 128 * d:128 * (d + 1)], psum[bank][:, 0:128]),
                  reads=(f"ps{bank}",), writes=(f"terns{c}",), dbg="ternT evac")

        widx = [0]
        gidx = [0]

        def wload(src_ap):
            i = widx[0] % RW
            widx[0] += 1
            dma_load(f"ringw{i}", ring_w[i][:], src_ap, stream=f"rw{i}")
            return i, f"ringw{i}"

        def gload(src_ap):
            i = gidx[0] % RG
            gidx[0] += 1
            dma_load(f"ringg{i}", ring_g[i][:], src_ap, stream=f"rg{i}")
            return i, f"ringg{i}"

        def ln_rs_bb(src_key, src_ap, dump_keys, dump_ap, i_s1, i_s2, i_rs, i_bb):
            s1, s2, rs, bb = smc(i_s1), smc(i_s2), smc(i_rs), smc(i_bb)
            P("dve", lambda E: E.reduce_sum(s1, src_ap, axis=mybir.AxisListType.X),
              reads=(src_key,), writes=(f"sm{i_s1}",), dbg="ln s1")
            P("act", lambda E: E.activation(dump_ap, src_ap, AF.Square),
              reads=(src_key,), writes=dump_keys, dbg="ln sq")
            P("dve", lambda E: E.reduce_sum(s2, dump_ap, axis=mybir.AxisListType.X),
              reads=dump_keys, writes=(f"sm{i_s2}",), dbg="ln s2")
            P("dve", lambda E: E.tensor_scalar(s1, s1, 1.0 / D, None, OP.mult),
              reads=(f"sm{i_s1}",), writes=(f"sm{i_s1}",), dbg="ln m")
            P("dve", lambda E: E.scalar_tensor_tensor(rs, s1, 1.0, s1, OP.mult, OP.mult),
              reads=(f"sm{i_s1}",), writes=(f"sm{i_rs}",), dbg="ln m2")
            P("dve", lambda E: E.scalar_tensor_tensor(s2, s2, 1.0 / D, rs, OP.mult, OP.subtract),
              reads=(f"sm{i_s2}", f"sm{i_rs}"), writes=(f"sm{i_s2}",), dbg="ln var")
            P("dve", lambda E: E.tensor_scalar(s2, s2, 1.0e-5, None, OP.add),
              reads=(f"sm{i_s2}",), writes=(f"sm{i_s2}",), dbg="ln +eps")
            P("act", lambda E: E.activation(rs, s2, AF.Ln),
              reads=(f"sm{i_s2}",), writes=(f"sm{i_rs}",), dbg="ln ln")
            P("act", lambda E: E.activation(rs, rs, AF.Exp, scale=cmhalf),
              reads=(f"sm{i_rs}", "cvec"), writes=(f"sm{i_rs}",), dbg="ln rsqrt")
            P("dve", lambda E: E.scalar_tensor_tensor(bb, s1, -1.0, rs, OP.mult, OP.mult),
              reads=(f"sm{i_s1}", f"sm{i_rs}"), writes=(f"sm{i_bb}",), dbg="ln bb")
            return rs, bb

        def fe_A_ln(g):
            xr = x_ring[g % XR]
            xkey = f"xr{g % XR}"
            dma_load(xkey, xr[:], x_hbm[g * 128:(g + 1) * 128, :])
            rs, bb = ln_rs_bb(xkey, xr[:], ("htm", "tfw"), h_tm[:], 0, 1, 2, 3)
            P("dve", lambda E, xr=xr, rs=rs, bb=bb: E.tensor_scalar(
                h_tm[:], xr[:], rs, bb, OP.mult, OP.add),
              reads=(xkey, "sm2", "sm3"), writes=("htm", "tfw"), dbg="ln1 apply")

        def fe_A_hT(g):
            col = (g % 2) * 128
            for half in range(2):
                bank = 2 + half
                for r in range(4):
                    c = half * 4 + r
                    P("pe", lambda E, c=c, bank=bank, r=r: E.matmul(
                        psum[bank][:, r * 128:(r + 1) * 128], h_tm[:, 128 * c:128 * (c + 1)],
                        ident[:], is_transpose=True, start=(r == 0), stop=(r == 3)),
                      reads=("htm", "ident"), writes=(f"ps{bank}",), dbg="hT")

                def evac(E, half=half, bank=bank, col=col):
                    last = None
                    for r in range(4):
                        c = half * 4 + r
                        last = E.copy(h_fm[c][:, col:col + 128],
                                      psum[bank][:, r * 128:(r + 1) * 128])
                    return last
                P("act", evac, reads=(f"ps{bank}",),
                  writes=tuple(f"hfm{half*4+r}" for r in range(4)), dbg="hT evac")
            if DBG:
                for c in range(8):
                    r0 = (g * 8 + c) * 128
                    dbg_store(f"hfm{c}", dbgfm_h[r0:r0 + 128, :], h_fm[c][:, col:col + 128])
            for c in range(8):
                lat0 = 4 * g
                P("dve", lambda E, c=c, col=col, lat0=lat0: E.reduce_sum(
                    pooled[c][:, lat0:lat0 + 4],
                    h_fm[c][:, col:col + 128].rearrange("p (s q) -> p s q", q=32),
                    axis=mybir.AxisListType.X),
                  reads=(f"hfm{c}",), writes=(f"pool{c}",), dbg="pool reduce")
                P("dve", lambda E, c=c, lat0=lat0: E.tensor_scalar(
                    pooled[c][:, lat0:lat0 + 4], pooled[c][:, lat0:lat0 + 4],
                    1.0 / CH, None, OP.mult),
                  reads=(f"pool{c}",), writes=(f"pool{c}",), dbg="pool scale")
            if g % 2 == 1:
                lat0 = 4 * (g - 1)
                for k in range(8):
                    i, keyg = gload(wgk_h[128 * k:128 * (k + 1), :])
                    for m in range(4):
                        P("pe", lambda E, i=i, k=k, m=m, lat0=lat0: E.matmul(
                            psum[0][:, 8 * m:8 * m + 8], ring_g[i][:, 128 * m:128 * (m + 1)],
                            pooled[k][:, lat0:lat0 + 8],
                            start=(k == 0 and m == 0), stop=(k == 7 and m == 3)),
                          reads=(keyg, f"pool{k}"), writes=("ps0",), dbg="GK mm")
                for m in range(4):
                    P("act", lambda E, m=m, lat0=lat0: E.copy(gk[m][:, lat0:lat0 + 8],
                                                              psum[0][:, 8 * m:8 * m + 8]),
                      reads=("ps0",), writes=(f"gk{m}",), dbg="GK evac")
                for k in range(8):
                    i, keyg = gload(wgv_h[128 * k:128 * (k + 1), :])
                    P("pe", lambda E, i=i, k=k: E.matmul(
                        psum[1][0:64, :], pooled[k][:, :], ring_g[i][:, :],
                        start=(k == 0), stop=(k == 7)),
                      reads=(keyg, f"pool{k}"), writes=("ps1",), dbg="GV mm")
                P("act", lambda E: E.copy(gvd[0:64, :], psum[1][0:64, :]),
                  reads=("ps1",), writes=("gvd",), dbg="GV evac")
                dma_load("gvd", gvd[64:128, :], gvd[0:64, :], extra_reads=("gvd",))

        def fe_A(g):
            fe_A_ln(g)
            fe_A_hT(g)

        def proj_pair(w_h, out_cb, dbgname):
            for k in range(8):
                i, keyw = wload(w_h[128 * k:128 * (k + 1), :])
                for m in range(8):
                    bk = m % 4
                    reg = 256 * (m // 4)
                    P("pe", lambda E, i=i, k=k, m=m, bk=bk, reg=reg: E.matmul(
                        psum[bk][:, reg:reg + 256], ring_w[i][:, 128 * m:128 * (m + 1)],
                        h_fm[k][:, 0:256],
                        start=(k == 0 and m < 4), stop=(k == 7 and m >= 4)),
                      reads=(keyw, f"hfm{k}"), writes=(f"ps{bk}",), dbg=dbgname)
            for m in range(8):
                out_cb(m, m % 4, 256 * (m // 4))

        def fe_Qpair(b0):
            def out_cb(m, bk, reg):
                P("act", lambda E, m=m, bk=bk, reg=reg: E.copy(
                    q_fm[m][:, 0:256], psum[bk][:, reg:reg + 256]),
                  reads=(f"ps{bk}",), writes=(f"qfm{m}",), dbg="Q evac")
            proj_pair(wq_h, out_cb, "Q mm")
            if DBG:
                for qi in range(2):
                    for c in range(8):
                        r0 = 16384 + ((b0 + qi) * 8 + c) * 128
                        dbg_store(f"qfm{c}", dbgfm_h[r0:r0 + 128, :],
                                  q_fm[c][:, 128 * qi:128 * qi + 128])

        def fe_KVpair(b0):
            s0 = b0 % KR

            def out_cb(m, bk, reg):
                P("act", lambda E, m=m, bk=bk, reg=reg: E.copy(
                    kb[m][:, 128 * s0:128 * s0 + 256], psum[bk][:, reg:reg + 256]),
                  reads=(f"ps{bk}",), writes=(f"kb{m}",), dbg="K evac")
            proj_pair(wk_h, out_cb, "K mm")
            if DBG:
                for qi in range(2):
                    for c in range(8):
                        r0 = 2 * 16384 + ((b0 + qi) * 8 + c) * 128
                        dbg_store(f"kb{c}", dbgfm_h[r0:r0 + 128, :],
                                  kb[c][:, 128 * (s0 + qi):128 * (s0 + qi) + 128])
            for k in range(8):
                i, keyw = wload(wv_h[128 * k:128 * (k + 1), :])
                for qi in range(2):
                    for half in range(2):
                        bk = qi * 2 + half
                        P("pe", lambda E, i=i, k=k, qi=qi, half=half, bk=bk: E.matmul(
                            psum[bk][:, :], h_fm[k][:, 128 * qi:128 * (qi + 1)],
                            ring_w[i][:, 512 * half:512 * (half + 1)],
                            start=(k == 0), stop=(k == 7)),
                          reads=(keyw, f"hfm{k}"), writes=(f"ps{bk}",), dbg="V mm")
            for qi in range(2):
                slot = (b0 + qi) % KR
                for half in range(2):
                    bk = qi * 2 + half
                    P("act", lambda E, half=half, slot=slot, bk=bk: E.copy(
                        vb[slot][:, 512 * half:512 * (half + 1)], psum[bk][:, :]),
                      reads=(f"ps{bk}",), writes=(f"vb{slot}",), dbg="V evac")
                if DBG:
                    g = b0 + qi
                    dbg_store(f"vb{slot}", dbgtm_h[g * 128:(g + 1) * 128, :], vb[slot][:])

        def fe_GQgpair(b0):
            for k in range(8):
                i, keyg = gload(wgq_h[128 * k:128 * (k + 1), :])
                for m in range(4):
                    P("pe", lambda E, i=i, k=k, m=m: E.matmul(
                        psum[m][:, 0:256], ring_g[i][:, 128 * m:128 * (m + 1)],
                        h_fm[k][:, 0:256], start=(k == 0), stop=(k == 7)),
                      reads=(keyg, f"hfm{k}"), writes=(f"ps{m}",), dbg="GQ mm")
            for m in range(4):
                P("act", lambda E, m=m: E.copy(gq[m][:, 0:256], psum[m][:, 0:256]),
                  reads=(f"ps{m}",), writes=(f"gq{m}",), dbg="GQ evac")
            for qi in range(2):
                g = b0 + qi
                col = 128 * qi
                for k in range(8):
                    P("pe", lambda E, k=k, col=col: E.matmul(
                        psum[5][:, 0:32], q_fm[k][:, col:col + 128],
                        gw_s[:, 32 * k:32 * (k + 1)], start=(k == 0), stop=(k == 7)),
                      reads=(f"qfm{k}", "gws"), writes=("ps5",), dbg="gates mm")
                gt = gT[g % 4]
                gkey = f"gT{g % 4}"
                P("act", lambda E, gt=gt: E.activation(gt[:], psum[5][:, 0:32], AF.Exp, scale=cm1),
                  reads=("ps5", "cvec"), writes=(gkey,), dbg="gates exp")
                P("dve", lambda E, gt=gt: E.tensor_scalar(gt[:], gt[:], 1.0, None, OP.add),
                  reads=(gkey,), writes=(gkey,), dbg="gates 1+")
                P("dve", lambda E, gt=gt: E.reciprocal(gt[:], gt[:]),
                  reads=(gkey,), writes=(gkey,), dbg="gates recip")
                if DBG:
                    dbg_store(gkey, dbgg_h[g * 128:(g + 1) * 128, :], gt[:])

        def be_heads(g, plist, after_pair=None):
            mask = maskv if g < 2 else mask2
            mkey = "maskv" if g < 2 else "mask2"
            col = (g % 2) * 128

            def emit_score_pair(p):
                bb0 = (0, 1) if p % 2 == 0 else (6, 7)
                for s_ in range(3):
                    slot = (g - 2 + s_) % KR
                    for hi in range(2):
                        rowb = 64 * hi
                        sbank = bb0[hi]
                        P("pe", lambda E, p=p, rowb=rowb, sbank=sbank, s_=s_, slot=slot: E.matmul(
                            psum[sbank][:, 128 * s_:128 * (s_ + 1)],
                            kb[p][rowb:rowb + 64, 128 * slot:128 * slot + 128],
                            q_fm[p][rowb:rowb + 64, col:col + 128],
                            start=True, stop=True, tile_position=(rowb, 0)),
                          reads=(f"qfm{p}", f"kb{p}"), writes=(f"ps{sbank}",), dbg="score")

            def emit_softmax(h, sbank):
                P("dve", lambda E, sbank=sbank, mask=mask: E.tensor_tensor(
                    psum[sbank][:, 0:384], psum[sbank][:, 0:384], mask[:], OP.add),
                  reads=(f"ps{sbank}", mkey), writes=(f"ps{sbank}",), dbg="mask add")
                P("act", lambda E, sbank=sbank, h=h: E.activation(
                    PTs[h % 2][:], psum[sbank][:, 0:384], AF.Exp, scale=c0125, bias=cm24),
                  reads=(f"ps{sbank}", "cvec"), writes=(f"PTs{h%2}",), dbg="exp")

            def emit_pv(h):
                lbank = 4 + h // 8
                for s_ in range(3):
                    slot = (g - 2 + s_) % KR
                    P("pe", lambda E, h=h, s_=s_, slot=slot, lbank=lbank: E.matmul(
                        psum[lbank][:, (h % 8) * 64:(h % 8) * 64 + 64],
                        PTs[h % 2][:, 128 * s_:128 * (s_ + 1)],
                        vb[slot][:, h * 64:h * 64 + 64],
                        start=(s_ == 0), stop=(s_ == 2)),
                      reads=(f"PTs{h%2}", f"vb{slot}"), writes=(f"ps{lbank}",), dbg="PV")
                    P("pe", lambda E, h=h, s_=s_: E.matmul(
                        psum[2][:, h:h + 1], PTs[h % 2][:, 128 * s_:128 * (s_ + 1)],
                        c1, start=(s_ == 0), stop=(s_ == 2)),
                      reads=(f"PTs{h%2}", "cvec"), writes=("ps2",), dbg="Psum mm")

            for p in plist:
                a, b = 2 * p, 2 * p + 1
                emit_score_pair(p)
                if p >= 1:
                    emit_pv(a - 2)
                    emit_pv(b - 2)
                bb0 = (0, 1) if p % 2 == 0 else (6, 7)
                emit_softmax(a, bb0[0])
                emit_softmax(b, bb0[1])
                if after_pair is not None:
                    after_pair()
            if plist[-1] == NH // 2 - 1:
                emit_pv(NH - 2)
                emit_pv(NH - 1)
                P("act", lambda E: E.copy(sm[:, 40:56], psum[2][:, 0:16]),
                  reads=("ps2",), writes=("smsums",), dbg="smsums evac")

        def be_mix(g, ktffn_cb=None, yffn_cb=None):
            gkey = f"gT{g % 4}"
            gt = gT[g % 4]
            col = (g % 2) * 128
            P("dve", lambda E: E.reciprocal(sm[:, 56:72], sm[:, 40:56]),
              reads=("smsums",), writes=("smfac",), dbg="recip")
            P("dve", lambda E, gt=gt: E.tensor_mul(
                sm[:, 56:72], sm[:, 56:72], bass.AP(gt, 0, [[32, 128], [2, 16]])),
              reads=("smfac", gkey), writes=("smfac",), dbg="gl*recip")
            for half in range(2):
                lb = 4 + half
                glview = bass.AP(sm, 56 + 8 * half, [[SMW, 128], [1, 8], [0, 64]])
                P("dve", lambda E, lb=lb, half=half, glview=glview: E.tensor_tensor(
                    mix_tm[:, 512 * half:512 * (half + 1)], psum[lb][:, :], glview, OP.mult),
                  reads=(f"ps{lb}", "smfac"), writes=("mixtm",), dbg="mix gl")
            for h in range(LH):
                c = h // 2
                rowb = (h % 2) * 64
                bank = h % 2
                r = h // 2
                P("pe", lambda E, c=c, rowb=rowb, bank=bank, r=r: E.matmul(
                    psum[bank][:, 64 * r:64 * (r + 1)],
                    gq[c][rowb:rowb + 64, col:col + 128],
                    gk[c][rowb:rowb + 64, :],
                    start=(r == 0), stop=(r == 3), tile_position=(rowb, 0)),
                  reads=(f"gq{c}", f"gk{c}"), writes=(f"ps{bank}",), dbg="lsc")
            for par in range(2):
                bank = par
                P("dve", lambda E, bank=bank, g=g: E.tensor_tensor(
                    psum[bank][:, 0:256], psum[bank][:, 0:256],
                    bass.AP(lms, 64 * g, [[1024, 128], [0, 4], [1, 64]]), OP.add),
                  reads=(f"ps{bank}", "lms"), writes=(f"ps{bank}",), dbg="lmask add")
                P("act", lambda E, bank=bank, par=par: E.activation(
                    bass.AP(lp, 64 * par, [[512, 128], [128, 4], [1, 64]]),
                    psum[bank][:, 0:256], AF.Exp, scale=c0125, bias=cm12),
                  reads=(f"ps{bank}", "cvec"), writes=("lp",), dbg="lexp")
            if ktffn_cb is not None:
                ktffn_cb()
            P("dve", lambda E: E.reduce_sum(sm[:, 8:16], lp[:].rearrange("p (h n) -> p h n", n=64),
                                            axis=mybir.AxisListType.X),
              reads=("lp",), writes=("sm8",), dbg="lsum")
            P("dve", lambda E: E.reciprocal(sm[:, 8:16], sm[:, 8:16]),
              reads=("sm8",), writes=("sm8",), dbg="lrecip")
            P("dve", lambda E: E.tensor_tensor(
                lp[:], lp[:], bass.AP(sm, 8, [[SMW, 128], [1, 8], [0, 64]]), OP.mult),
              reads=("lp", "sm8"), writes=("lp",), dbg="lp scale")
            def lpT_ap(cc):
                if cc < 2:
                    return lpT[cc][:], f"lpT{cc}"
                return PTs[cc - 2][:, 256:384], f"PTs{cc - 2}"
            for cc in range(4):
                bank = 2 + cc % 2
                P("pe", lambda E, cc=cc, bank=bank: E.matmul(
                    psum[bank][:, 0:128], lp[:, 128 * cc:128 * (cc + 1)], ident[:],
                    is_transpose=True, start=True, stop=True),
                  reads=("lp", "ident"), writes=(f"ps{bank}",), dbg="lpT")
                ap_, key_ = lpT_ap(cc)
                P("act", lambda E, ap_=ap_, bank=bank: E.copy(ap_, psum[bank][:, 0:128]),
                  reads=(f"ps{bank}",), writes=(key_,), dbg="lpT evac")
            for cc in range(4):
                ap_, key_ = lpT_ap(cc)
                P("pe", lambda E, ap_=ap_, cc=cc: E.matmul(
                    psum[0][0:64, 128 * cc:128 * cc + 128],
                    gvd[0:64, 128 * cc:128 * cc + 64], ap_[0:64, :],
                    start=True, stop=True),
                  reads=("gvd", key_), writes=("ps0",), dbg="rem A")
                P("pe", lambda E, ap_=ap_, cc=cc: E.matmul(
                    psum[0][64:128, 128 * cc:128 * cc + 128],
                    gvd[64:128, 128 * cc + 64:128 * cc + 128], ap_[64:128, :],
                    start=True, stop=True, tile_position=(64, 64)),
                  reads=("gvd", key_), writes=("ps0",), dbg="rem B")
            for ph in range(2):
                P("act", lambda E, ph=ph: E.copy(rem_fm[:, 256 * ph:256 * (ph + 1)],
                                                 psum[0][:, 256 * ph:256 * (ph + 1)]),
                  reads=("ps0",), writes=("remfm", f"remfm{ph}"), dbg="rem evac")
            for k in range(4):
                for half in range(2):
                    b = 2 + half
                    P("pe", lambda E, k=k, half=half, b=b: E.matmul(
                        psum[b][:, :], rem_fm[:, 128 * k:128 * (k + 1)],
                        rout_s[k][:, 512 * half:512 * (half + 1)],
                        start=(k == 0), stop=(k == 3)),
                      reads=(f"remfm{k//2}", f"routs{k}"), writes=(f"ps{b}",), dbg="remrout")
            if yffn_cb is not None:
                yffn_cb()
            for half in range(2):
                b = 2 + half
                grview = bass.AP(gt, 1 + 16 * half, [[32, 128], [2, 8], [0, 64]])
                P("dve", lambda E, b=b, half=half, grview=grview: E.tensor_tensor(
                    qD[:, 512 * half:512 * (half + 1)], psum[b][:, :], grview, OP.mult),
                  reads=(f"ps{b}", gkey), writes=QD, dbg="mix gr")
                P("dve", lambda E, half=half: E.tensor_tensor(
                    mix_tm[:, 512 * half:512 * (half + 1)],
                    mix_tm[:, 512 * half:512 * (half + 1)],
                    qD[:, 512 * half:512 * (half + 1)], OP.add),
                  reads=("mixtm",) + QD, writes=("mixtm",), dbg="mix add")
            if DBG:
                dbg_store("mixtm", dbgtm_h[3 * S + g * 128:3 * S + (g + 1) * 128, :], mix_tm[:])
        def be_wout(g):
            for half in range(2):
                bank = 2 + half
                for r in range(4):
                    c = half * 4 + r
                    P("pe", lambda E, c=c, bank=bank, r=r: E.matmul(
                        psum[bank][:, r * 128:(r + 1) * 128], mix_tm[:, 128 * c:128 * (c + 1)],
                        ident[:], is_transpose=True, start=(r == 0), stop=(r == 3)),
                      reads=("mixtm", "ident"), writes=(f"ps{bank}",), dbg="mixT")

                def mevac(E, half=half, bank=bank):
                    last = None
                    for r in range(4):
                        c = half * 4 + r
                        last = E.copy(mix_fm[:, 128 * c:128 * (c + 1)],
                                      psum[bank][:, r * 128:(r + 1) * 128])
                    return last
                P("act", mevac, reads=(f"ps{bank}",), writes=("mixfm", f"mixfm{half}"),
                  dbg="mixT evac")
            for k in range(8):
                for half in range(2):
                    b = 4 + half
                    P("pe", lambda E, k=k, half=half, b=b: E.matmul(
                        psum[b][:, :], mix_fm[:, 128 * k:128 * (k + 1)],
                        wout_s[k][:, 512 * half:512 * (half + 1)],
                        start=(k == 0), stop=(k == 7)),
                      reads=(f"mixfm{k//4}", f"wouts{k}"), writes=(f"ps{b}",), dbg="wout")

        n_stores = [0]

        def epi_evac(g):
            for half in range(2):
                b = 4 + half
                P("act", lambda E, b=b, half=half: E.copy(
                    attnout[:, 512 * half:512 * (half + 1)], psum[b][:, :]),
                  reads=(f"ps{b}",), writes=("mixfm", f"mixfm{half}"), dbg="attnout evac")
                P("act", lambda E, b=b, half=half: E.activation(
                    qA[:, 512 * half:512 * (half + 1)], psum[b][:, :], AF.Square),
                  reads=(f"ps{b}",), writes=(f"qA{half}",), dbg="ln2 sq")
            if DBG:
                dbg_store("mixfm", dbgtm_h[S + g * 128:S + (g + 1) * 128, :], attnout[:])

        def epi_segs(g):
            segs = []

            def Sg(eng, fn, **kw):
                segs.append(lambda eng=eng, fn=fn, kw=kw: P(eng, fn, **kw))

            xr = x_ring[g % XR]
            xkey = f"xr{g % XR}"
            Sg("dve", lambda E: E.reduce_sum(smc(24), attnout[:], axis=mybir.AxisListType.X),
               reads=("mixfm",), writes=("sm24",), dbg="ln2 s1")
            Sg("dve", lambda E: E.reduce_sum(smc(25), qA[:], axis=mybir.AxisListType.X),
               reads=QA, writes=("sm25",), dbg="ln2 s2")
            Sg("dve", lambda E: E.tensor_scalar(smc(24), smc(24), 1.0 / D, None, OP.mult),
               reads=("sm24",), writes=("sm24",), dbg="ln2 m")
            Sg("dve", lambda E: E.scalar_tensor_tensor(smc(26), smc(24), 1.0, smc(24), OP.mult, OP.mult),
               reads=("sm24",), writes=("sm26",), dbg="ln2 m2")
            Sg("dve", lambda E: E.scalar_tensor_tensor(smc(25), smc(25), 1.0 / D, smc(26), OP.mult, OP.subtract),
               reads=("sm25", "sm26"), writes=("sm25",), dbg="ln2 var")
            Sg("dve", lambda E: E.tensor_scalar(smc(25), smc(25), 1.0e-5, None, OP.add),
               reads=("sm25",), writes=("sm25",), dbg="ln2 +eps")
            Sg("act", lambda E: E.activation(smc(26), smc(25), AF.Ln),
               reads=("sm25",), writes=("sm26",), dbg="ln2 ln")
            Sg("act", lambda E: E.activation(smc(26), smc(26), AF.Exp, scale=cmhalf),
               reads=("sm26", "cvec"), writes=("sm26",), dbg="ln2 rsqrt")
            Sg("dve", lambda E: E.scalar_tensor_tensor(smc(27), smc(24), -1.0, smc(26), OP.mult, OP.mult),
               reads=("sm24", "sm26"), writes=("sm27",), dbg="ln2 bb")
            Sg("dve", lambda E: E.tensor_scalar(qD[:], attnout[:], smc(26), smc(27), OP.mult, OP.add),
               reads=("mixfm", "sm26", "sm27"), writes=QD, dbg="ln2 apply")
            Sg("dve", lambda E, xr=xr: E.tensor_add(y1[:], qD[:], xr[:]),
               reads=QD + (xkey,), writes=("y1",), dbg="y1")
            y1_pos = len(segs)
            if DBG:
                segs.append(lambda g=g: dbg_store(
                    "y1", dbgtm_h[2 * S + g * 128:2 * S + (g + 1) * 128, :], y1[:]))
            Sg("dve", lambda E: E.reduce_sum(smc(28), y1[:], axis=mybir.AxisListType.X),
               reads=("y1",), writes=("sm28",), dbg="ln3 s1")
            Sg("dve", lambda E: E.tensor_scalar(smc(28), smc(28), 1.0 / D, None, OP.mult),
               reads=("sm28",), writes=("sm28",), dbg="ln3 m")
            Sg("dve", lambda E: E.tensor_scalar(qD[:], y1[:], smc(28), None, OP.subtract),
               reads=("y1", "sm28"), writes=QD, dbg="u")
            Sg("dve", lambda E: E.reduce_max(smc(31), qD[:], axis=mybir.AxisListType.X,
                                             apply_absolute_value=True),
               reads=QD, writes=("sm31",), dbg="amax_u")
            Sg("dve", lambda E: E.reciprocal(smc(32), smc(31)),
               reads=("sm31",), writes=("sm32",), dbg="amax recip")
            for half in range(2):
                hs = slice(512 * half, 512 * (half + 1))
                Sg("dve", lambda E, hs=hs: E.tensor_scalar(qA[:, hs], qD[:, hs], smc(32), 127.0,
                                                           OP.mult, OP.mult),
                   reads=(QD[half], "sm32"), writes=(QA[half],), dbg="m=u*s")
                Sg("gps", lambda E, hs=hs: E.tensor_copy(qI[:, hs], qA[:, hs]),
                   reads=(QA[half],), writes=(QI[half],), dbg="k0 rint")
            for half in range(2):
                hs = slice(512 * half, 512 * (half + 1))
                Sg("dve", lambda E, hs=hs: E.tensor_copy(qD[:, hs], qI[:, hs]),
                   reads=(QI[half],), writes=(QD[half],), dbg="k0f")
                Sg("dve", lambda E, hs=hs: E.tensor_sub(qA[:, hs], qA[:, hs], qD[:, hs]),
                   reads=(QA[half], QD[half]), writes=(QA[half],), dbg="d")
                Sg("dve", lambda E, hs=hs: E.scalar_tensor_tensor(qD[:, hs], qA[:, hs], 0.5,
                                                                  qD[:, hs], OP.is_gt, OP.add),
                   reads=(QA[half], QD[half]), writes=(QD[half],), dbg="k1")
                Sg("dve", lambda E, hs=hs: E.tensor_scalar(qA[:, hs], qA[:, hs], -0.5, None, OP.is_lt),
                   reads=(QA[half],), writes=(QA[half],), dbg="e2")
                Sg("dve", lambda E, hs=hs: E.tensor_sub(qD[:, hs], qD[:, hs], qA[:, hs]),
                   reads=(QD[half], QA[half]), writes=(QD[half],), dbg="k final")
            Sg("act", lambda E: E.activation(mix_tm[:], y1[:], AF.Square),
               reads=("y1",), writes=("mixtm",), dbg="ln3 sq")
            Sg("dve", lambda E: E.reduce_sum(smc(29), mix_tm[:], axis=mybir.AxisListType.X),
               reads=("mixtm",), writes=("sm29",), dbg="ln3 s2")
            Sg("dve", lambda E: E.scalar_tensor_tensor(smc(30), smc(28), 1.0, smc(28), OP.mult, OP.mult),
               reads=("sm28",), writes=("sm30",), dbg="ln3 m2")
            Sg("dve", lambda E: E.scalar_tensor_tensor(smc(29), smc(29), 1.0 / D, smc(30), OP.mult, OP.subtract),
               reads=("sm29", "sm30"), writes=("sm29",), dbg="ln3 var")
            Sg("dve", lambda E: E.tensor_scalar(smc(29), smc(29), 1.0e-5, None, OP.add),
               reads=("sm29",), writes=("sm29",), dbg="ln3 +eps")
            Sg("act", lambda E: E.activation(smc(30), smc(29), AF.Ln),
               reads=("sm29",), writes=("sm30",), dbg="ln3 ln")
            Sg("act", lambda E: E.activation(smc(30), smc(30), AF.Exp, scale=cmhalf),
               reads=("sm30", "cvec"), writes=("sm30",), dbg="ln3 rsqrt")
            Sg("dve", lambda E: E.tensor_mul(smc(33), smc(31), smc(30)),
               reads=("sm31", "sm30"), writes=("sm33",), dbg="amax_h2")
            Sg("dve", lambda E: E.tensor_scalar(smc(33), smc(33), 1.0e-5, None, OP.max),
               reads=("sm33",), writes=("sm33",), dbg="amax clamp")
            Sg("dve", lambda E: E.tensor_scalar(smc(33), smc(33), ws_col[:], 1.0 / 127.0,
                                                OP.mult, OP.mult),
               reads=("sm33", "wscol"), writes=("sm33",), dbg="fscale")
            return segs, y1_pos

        def epi_ktffn(g):
            for half in range(2):
                bank = 2 + half
                for r in range(4):
                    c = half * 4 + r
                    P("pe", lambda E, c=c, bank=bank, r=r: E.matmul(
                        psum[bank][:, r * 128:(r + 1) * 128], qD[:, 128 * c:128 * (c + 1)],
                        ident[:], is_transpose=True, start=(r == 0), stop=(r == 3)),
                      reads=(QD[half], "ident"), writes=(f"ps{bank}",), dbg="kT")

                def kevac(E, half=half, bank=bank):
                    last = None
                    for r in range(4):
                        c = half * 4 + r
                        last = E.copy(kfm[c][:], psum[bank][:, r * 128:(r + 1) * 128])
                    return last
                P("act", kevac, reads=(f"ps{bank}",),
                  writes=tuple(f"kfm{half*4+r}" for r in range(4)), dbg="kT evac")
            for k in range(8):
                for half in range(2):
                    b = 6 + half
                    P("pe", lambda E, k=k, half=half, b=b: E.matmul(
                        psum[b][:, :], kfm[k][:], tern_s[k][:, 512 * half:512 * (half + 1)],
                        start=(k == 0), stop=(k == 7)),
                      reads=(f"kfm{k}", f"terns{k}"), writes=(f"ps{b}",), dbg="ffn mm")

        def epi_yffn(g):
            for half in range(2):
                b = 6 + half
                P("dve", lambda E, b=b, half=half: E.scalar_tensor_tensor(
                    qA[:, 512 * half:512 * (half + 1)], psum[b][:, :], smc(33),
                    bias_bc[:, 512 * half:512 * (half + 1)], OP.mult, OP.add),
                  reads=(f"ps{b}", "sm33", "biasbc"), writes=(QA[half],), dbg="yffn")

        def epi_pe_mm(g):
            epi_ktffn(g)
            epi_yffn(g)

        def epi_ln4(g):
            P("dve", lambda E: E.reduce_sum(smc(34), qA[:], axis=mybir.AxisListType.X),
              reads=QA, writes=("sm34",), dbg="ln4 s1")
            P("act", lambda E: E.activation(qD[:], qA[:], AF.Square),
              reads=QA, writes=QD, dbg="ln4 sq")
            P("dve", lambda E: E.reduce_sum(smc(35), qD[:], axis=mybir.AxisListType.X),
              reads=QD, writes=("sm35",), dbg="ln4 s2")
            P("dve", lambda E: E.tensor_scalar(smc(34), smc(34), 1.0 / D, None, OP.mult),
              reads=("sm34",), writes=("sm34",), dbg="ln4 m")
            P("dve", lambda E: E.scalar_tensor_tensor(smc(36), smc(34), 1.0, smc(34), OP.mult, OP.mult),
              reads=("sm34",), writes=("sm36",), dbg="ln4 m2")
            P("dve", lambda E: E.scalar_tensor_tensor(smc(35), smc(35), 1.0 / D, smc(36), OP.mult, OP.subtract),
              reads=("sm35", "sm36"), writes=("sm35",), dbg="ln4 var")
            P("dve", lambda E: E.tensor_scalar(smc(35), smc(35), 1.0e-5, None, OP.add),
              reads=("sm35",), writes=("sm35",), dbg="ln4 +eps")
            P("act", lambda E: E.activation(smc(36), smc(35), AF.Ln),
              reads=("sm35",), writes=("sm36",), dbg="ln4 ln")
            P("act", lambda E: E.activation(smc(36), smc(36), AF.Exp, scale=cmhalf),
              reads=("sm36", "cvec"), writes=("sm36",), dbg="ln4 rsqrt")
            P("dve", lambda E: E.scalar_tensor_tensor(smc(37), smc(34), -1.0, smc(36), OP.mult, OP.mult),
              reads=("sm34", "sm36"), writes=("sm37",), dbg="ln4 bb")
            P("dve", lambda E: E.tensor_scalar(qD[:], qA[:], smc(36), smc(37), OP.mult, OP.add),
              reads=QA + ("sm36", "sm37"), writes=QD, dbg="ln4 apply")

        def epi_y2(g):
            P("dve", lambda E: E.tensor_add(attnout[:], qD[:], y1[:]),
              reads=QD + ("y1",), writes=("mixfm", "mixfm0", "mixfm1"), dbg="y2")
            n_stores[0] += 1
            dma_store("mixfm", y_hbm[g * 128:(g + 1) * 128, :], attnout[:])

        pend = {"segs": [], "y1pos": 0, "done": 0}

        def drain(n):
            s = pend["segs"]
            stop = min(pend["done"] + n, len(s))
            for i in range(pend["done"], stop):
                s[i]()
            pend["done"] = stop

        if SCHED == "seq":
            for b0 in range(0, NQT, 2):
                fe_A(b0)
                fe_A(b0 + 1)
                fe_Qpair(b0)
                fe_KVpair(b0)
                fe_GQgpair(b0)
                for g in (b0, b0 + 1):
                    be_heads(g, list(range(0, 4)))
                    be_heads(g, list(range(4, 8)))
                    if g == 0:
                        dma_load("maskv", maskv[:], m1_h[:, :])
                    be_mix(g)
                    be_wout(g)
                    epi_evac(g)
                    pend["segs"], pend["y1pos"] = epi_segs(g)
                    pend["done"] = 0
                    drain(10**9)
                    epi_pe_mm(g)
                    epi_ln4(g)
                    epi_y2(g)
        else:
            fe_A(0)
            fe_A(1)
            fe_Qpair(0)
            fe_KVpair(0)
            fe_GQgpair(0)
            dr5 = lambda: drain(5)
            for g in range(NQT):
                be_heads(g, list(range(0, 2)), after_pair=dr5)
                if g + 2 < NQT:
                    fe_A_ln(g + 2)
                be_heads(g, list(range(2, 4)), after_pair=dr5)
                be_heads(g, list(range(4, 8)), after_pair=dr5)
                if g == 0:
                    dma_load("maskv", maskv[:], m1_h[:, :])
                drain(10**9)
                if g + 2 < NQT:
                    fe_A_hT(g + 2)

                def fill(g=g):
                    if g % 2 == 1 and g + 2 < NQT:
                        fe_Qpair(g + 1)
                    if g >= 1:
                        epi_ktffn(g - 1)
                ycb = (lambda gg=g - 1: epi_yffn(gg)) if g >= 1 else None
                be_mix(g, ktffn_cb=fill, yffn_cb=ycb)
                if g >= 1:
                    epi_ln4(g - 1)
                be_wout(g)
                if g >= 1:
                    epi_y2(g - 1)
                epi_evac(g)
                if g % 2 == 1 and g + 2 < NQT:
                    fe_KVpair(g + 1)
                    fe_GQgpair(g + 1)
                pend["segs"], pend["y1pos"] = epi_segs(g)
                pend["done"] = 0
            drain(10**9)
            epi_pe_mm(NQT - 1)
            epi_ln4(NQT - 1)
            epi_y2(NQT - 1)

        eng_counts = {e: 0 for e in ("pe", "dve", "act", "gps")}
        stream_counts = {}
        stream_eng = {}
        needed = set()
        for n in plan.nodes:
            for d in n.waits.values():
                needed.add(id(d))
        for n in plan.nodes:
            if n.is_dma:
                c = stream_counts.get(n.stream, 0) + 16
                stream_counts[n.stream] = c
                stream_eng[n.stream] = n.eng
                n.inc_val = (n.stream, c)
            elif id(n) in needed:
                eng_counts[n.eng] += 1
                n.inc_val = (n.eng, eng_counts[n.eng])

        with nc.Block() as block:
            def emit_engine(eng, E):
                waited = {}
                for n in plan.by_eng[eng]:
                    for d in n.waits.values():
                        iv = d.inc_val
                        assert iv is not None, f"dep of {n.dbg} on {d.dbg} has no inc"
                        semname, val = iv
                        if waited.get(semname, 0) >= val:
                            continue
                        waited[semname] = val
                        E.wait_ge(sems[semname], val)
                    instr = n.fn(E)
                    if n.inc_val is not None:
                        semname, val = n.inc_val
                        instr.then_inc(sems[semname], 16 if n.is_dma else 1)
                for s_, cnt in stream_counts.items():
                    if stream_eng[s_] == eng:
                        E.wait_ge(sems[s_], cnt)

            @block.tensor
            def _(E):
                emit_engine("pe", E)

            @block.vector
            def _(E):
                emit_engine("dve", E)

            @block.scalar
            def _(E):
                emit_engine("act", E)

            @block.gpsimd
            def _(E):
                emit_engine("gps", E)

            @block.sync
            def _(E):
                emit_engine("sp", E)

    return nc


def _lmask_table():
    import ml_dtypes
    out = np.empty((2048, 64), dtype=ml_dtypes.bfloat16)
    for g in range(16):
        r = np.arange(128)[:, None]
        n = np.arange(64)[None, :]
        out[128 * g:128 * (g + 1)] = np.where(n <= 4 * g + r // 32, 0.0, NEG)
    return out


def _host_constants():
    import ml_dtypes
    ident = np.eye(128, dtype=np.float32)
    r = np.arange(128)[:, None]
    j = np.arange(384)[None, :]
    base = (j - r >= 1) & (j - r <= 256)
    m2 = np.where(base, 0.0, NEG).astype(ml_dtypes.bfloat16)
    m1 = np.where(base & (j >= 128), 0.0, NEG).astype(ml_dtypes.bfloat16)
    m0 = np.where(base & (j >= 256), 0.0, NEG).astype(ml_dtypes.bfloat16)

    def tr(m):
        out = np.empty_like(m)
        for s in range(3):
            out[:, 128 * s:128 * (s + 1)] = m[:, 128 * s:128 * (s + 1)].T
        return out
    return ident, tr(m0), tr(m1), tr(m2)


def kernel(L0_attn_x, L0_attn_wq, L0_attn_wk, L0_attn_wv, L0_attn_wgq, L0_attn_wgk,
           L0_attn_wgv, L0_attn_r_out_w, L0_attn_gate_local_w, L0_attn_gate_remote_w,
           L0_attn_w_out, L0_ffn_packed_weight, L0_ffn_weight_scale, L0_ffn_bias):
    x = np.ascontiguousarray(np.asarray(L0_attn_x, dtype=np.float32))
    ident, m0, m1, m2 = _host_constants()
    glw = np.asarray(L0_attn_gate_local_w, dtype=np.float32)
    grw = np.asarray(L0_attn_gate_remote_w, dtype=np.float32)
    gwbd = np.zeros((D, 32), dtype=np.float32)
    for h in range(NH):
        gwbd[64 * h:64 * (h + 1), 2 * h] = glw[h]
        gwbd[64 * h:64 * (h + 1), 2 * h + 1] = grw[h]
    common = dict(
        wq=np.ascontiguousarray(L0_attn_wq, dtype=np.float32),
        wk=np.ascontiguousarray(L0_attn_wk, dtype=np.float32),
        wv=np.ascontiguousarray(L0_attn_wv, dtype=np.float32),
        wgq=np.ascontiguousarray(L0_attn_wgq, dtype=np.float32),
        wgk=np.ascontiguousarray(L0_attn_wgk, dtype=np.float32),
        wgv=np.ascontiguousarray(L0_attn_wgv, dtype=np.float32),
        rout=np.ascontiguousarray(L0_attn_r_out_w, dtype=np.float32),
        wout=np.ascontiguousarray(L0_attn_w_out, dtype=np.float32),
        gwbd=gwbd,
        packed=np.ascontiguousarray(L0_ffn_packed_weight, dtype=np.int32),
        wscale=np.ascontiguousarray(L0_ffn_weight_scale, dtype=np.float32),
        fbias=np.ascontiguousarray(L0_ffn_bias, dtype=np.float32),
        ident=ident, m0=m0, m1=m1, m2=m2,
        zeros=np.zeros((128, 1024), dtype=np.float32),
        cvec=np.tile(np.array([0.125, -24.0, -12.0, -0.5, 0.0, -1.0, 1.0, 0.0],
                              dtype=np.float32), (128, 1)),
        lmasks=_lmask_table(),
    )
    in_maps = [dict(common, x=np.ascontiguousarray(x[b])) for b in range(8)]
    nc = build_program()
    res = run_bass_kernel_spmd(nc, in_maps, core_ids=list(range(8)))
    out = np.stack([np.asarray(r["y"], dtype=np.float32) for r in res.results], axis=0)
    return out
